# version 07
# baseline (speedup 1.0000x reference)
import math
import contextlib
import numpy as np
import ml_dtypes
import concourse.bass as bass
import concourse.mybir as mybir
from concourse.bass_utils import run_bass_kernel_spmd

F32 = mybir.dt.float32
BF16 = mybir.dt.bfloat16
I8 = mybir.dt.int8
AF = mybir.ActivationFunctionType
ALU = mybir.AluOpType
AX = mybir.AxisListType

NCORES = 8
B, L, D = 2, 4096, 2048
H, DH = 16, 128
RQ, RKV = 1536, 512
INTER = 8192
NT = 1024
SCALE = 1.0 / math.sqrt(DH)
_ABASE = 2.0 ** (-2.0 ** (-(math.log2(H) - 3)))
SLOPES = [_ABASE ** (i + 1) for i in range(H)]
EPS = 1e-5
KILL = -1.0e30

ENGS = ("tensor", "vector", "scalar", "gpsimd", "sync")
GROUPS = [[0, 1, 2, 3], [4, 5, 6, 7]]

NSG = {0: 28, 1: 32}
NS = {0: (16, 32), 1: (32, 32)}


def stripes_of(m):
    return (m, 7 - m)


class Bld:
    def __init__(self, nc, stack):
        self.nc = nc
        self.sem = {}
        self.c = {}
        for name in ("bar", "dg", "ds", "dq", "mm", "vv", "aa", "cc", "sxv", "sxa", "mr"):
            self.sem[name] = stack.enter_context(nc.semaphore(name))
            self.c[name] = 0

    def sx(self, ins, which="sxv"):
        self.c[which] += 1
        ins.then_inc(self.sem[which], 1)
        return self.c[which]

    def phase(self, fns):
        target = self.c["bar"]
        with self.nc.Block() as blk:
            for eng in ENGS:
                fn = fns.get(eng)

                def wrap(e, fn=fn):
                    e.wait_ge(self.sem["bar"], target)
                    if fn is not None:
                        fn(e)
                    e.drain()
                    e.sem_inc(self.sem["bar"], 1)

                getattr(blk, eng)(wrap)
        self.c["bar"] += 5


def build_program(dbg=()):
    nc = bass.Bass("TRN2", target_bir_lowering=False)
    dt = nc.dram_tensor

    xT = dt("xT", [D, NT], F32, kind="ExternalInput")
    wqdT = dt("wqdT", [D, RQ], F32, kind="ExternalInput")
    wqupT = dt("wqupT", [RQ, D], F32, kind="ExternalInput")
    wkvdT = dt("wkvdT", [D, RKV], F32, kind="ExternalInput")
    kupw = dt("kupw", [H, RKV, DH], F32, kind="ExternalInput")
    vupflat = dt("vupflat", [RKV, H * DH], F32, kind="ExternalInput")
    woT0 = dt("woT0", [D, D], F32, kind="ExternalInput")
    wg8 = dt("wg8", [64, 128, 16, 128], I8, kind="ExternalInput")
    wu8 = dt("wu8", [64, 128, 16, 128], I8, kind="ExternalInput")
    wd8 = dt("wd8", [INTER, D], I8, kind="ExternalInput")
    sgate = dt("sgate", [128, 64], F32, kind="ExternalInput")
    supv = dt("supv", [128, 64], F32, kind="ExternalInput")
    sdownp = dt("sdownp", [128, 16], F32, kind="ExternalInput")
    wq1 = dt("wq1", [D, D], F32, kind="ExternalInput")
    wk1 = dt("wk1", [D, D], F32, kind="ExternalInput")
    wv1 = dt("wv1", [D, D], F32, kind="ExternalInput")
    wo1 = dt("wo1", [D, D], F32, kind="ExternalInput")
    wT1 = dt("wT1", [D, D], F32, kind="ExternalInput")
    bP = dt("bP", [128, 16], F32, kind="ExternalInput")
    aTw = dt("aTw", [D, 128], F32, kind="ExternalInput")
    bstackT = dt("bstackT", [128, D], F32, kind="ExternalInput")
    gwT = dt("gwT", [D, 8], F32, kind="ExternalInput")
    gbrep = dt("gbrep", [128, 8], F32, kind="ExternalInput")
    rampt = dt("rampt", [128, 2048], BF16, kind="ExternalInput")
    ropeq = dt("ropeq", [128, 2 * NT], BF16, kind="ExternalInput")
    ropek = dt("ropek", [128, 2 * NT], BF16, kind="ExternalInput")
    alibi = dt("alibi", [128, H * 48], F32, kind="ExternalInput")
    masksw = dt("masksw", [128, 4 * 512], mybir.dt.uint8, kind="ExternalInput")
    pmatw = dt("pmatw", [128, 128], BF16, kind="ExternalInput")
    eselw = dt("eselw", [8, 128], BF16, kind="ExternalInput")
    idnw = dt("idnw", [128, 128], BF16, kind="ExternalInput")
    ones1 = dt("ones1", [1, 128], BF16, kind="ExternalInput")
    onesm = dt("onesm", [128, 128], BF16, kind="ExternalInput")
    onescol = dt("onescol", [128, 1], BF16, kind="ExternalInput")
    invd = dt("invd", [128, 1], BF16, kind="ExternalInput")
    epsap = dt("epsap", [1, 1], F32, kind="ExternalInput")

    outx = dt("outx", [D, NT], F32, kind="ExternalOutput")
    dbg_list = list(dbg)
    dbg_out = {}

    ag_kin = [dt(f"ag_kin{l}", [D, NT], BF16) for l in range(2)]
    ag_kout = [dt(f"ag_kout{l}", [4 * D, NT], BF16) for l in range(2)]
    ag_vin = [dt(f"ag_vin{l}", [NT, D], BF16) for l in range(2)]
    ag_vout = [dt(f"ag_vout{l}", [4 * NT, D], BF16) for l in range(2)]

    with contextlib.ExitStack() as S:
        b = Bld(nc, S)
        sem, c = b.sem, b.c
        _names = {}

        def sb(name, shape, dtype):
            n = _names.get(name, 0)
            _names[name] = n + 1
            return nc.sbuf_tensor(f"{name}_{n}" if n else name, shape, dtype)

        XF = S.enter_context(sb("XF", [128, 16 * NT], F32))
        XB = S.enter_context(sb("XB", [128, 16 * NT], BF16))
        SC2 = S.enter_context(sb("SC2", [128, 512], F32))
        SC3 = S.enter_context(sb("SC3", [128, 512], F32))
        CST = S.enter_context(sb("CST", [128, 800], F32))
        CSB = S.enter_context(sb("CSB", [128, 512], BF16))
        MSK8 = S.enter_context(sb("MSK8", [128, 2048], mybir.dt.uint8))
        NEGC = S.enter_context(sb("NEGC", [128, 512], BF16))
        ONESM = S.enter_context(sb("ONESM", [128, 128], BF16))
        E8 = S.enter_context(sb("E8", [8, 128 + NT], BF16))
        RCPB = S.enter_context(sb("RCPB", [1, 512], BF16))
        SDN = S.enter_context(sb("SDN", [128, 32], F32))

        pb = [S.enter_context(nc.psum_tensor(f"pb{i}", [128, 512], F32))
              for i in range(4)]

        EPSV = CST[0:1, 768:769]
        MASK = [MSK8[:, 512 * i:512 * (i + 1)] for i in range(4)]
        PMAT = CSB[:, 0:128]
        ONESC = CSB[:, 128:129]
        INVD = CSB[:, 129:130]
        IDN = CSB[:, 130:258]
        ONESR = CSB[0:1, 258:386]
        ALB = CST[:, 0:768]

        def p0_sync(e):
            e.dma_start(XF[:].rearrange("p (ch t) -> p ch t", ch=16),
                        xT.ap().rearrange("(ch p) t -> p ch t", p=128)).then_inc(sem["ds"], 16)
            e.dma_start(CST[:, 0:768], alibi.ap()).then_inc(sem["ds"], 16)
            e.dma_start(CST[0:1, 768:769], epsap.ap()).then_inc(sem["ds"], 16)
            e.dma_start(MSK8[:, 0:2048], masksw.ap()).then_inc(sem["ds"], 16)
            e.dma_start(CSB[:, 0:128], pmatw.ap()).then_inc(sem["ds"], 16)
            e.dma_start(CSB[:, 128:129], onescol.ap()).then_inc(sem["ds"], 16)
            e.dma_start(CSB[:, 129:130], invd.ap()).then_inc(sem["ds"], 16)
            e.dma_start(CSB[:, 130:258], idnw.ap()).then_inc(sem["ds"], 16)
            e.dma_start(CSB[0:1, 258:386], ones1.ap()).then_inc(sem["ds"], 16)
            e.dma_start(E8[:, 0:128], eselw.ap()).then_inc(sem["ds"], 16)
            e.dma_start(ONESM[:], onesm.ap()).then_inc(sem["ds"], 16)
            e.dma_start(SDN[:, 0:16], sdownp.ap()).then_inc(sem["ds"], 16)
            e.dma_start(SDN[:, 16:32], bP.ap()).then_inc(sem["ds"], 16)
            e.wait_ge(sem["ds"], c["ds"] + 13 * 16)

        def p0_g(e):
            e.dma_start(XB[:].rearrange("p (ch t) -> p ch t", ch=16),
                        xT.ap().rearrange("(ch p) t -> p ch t", p=128)).then_inc(sem["dg"], 16)
            e.wait_ge(sem["dg"], c["dg"] + 16)

        def p0_v(e):
            e.memset(NEGC[:, :], -1.0e30)

        b.phase({"sync": p0_sync, "gpsimd": p0_g, "vector": p0_v})
        c["ds"] += 13 * 16
        c["dg"] += 16

        def xb_rhs(dc, tn):
            return XB[:, dc * NT + 512 * tn: dc * NT + 512 * tn + 512]

        def emit_proj(wd_ap, Kc, Mt, rhs_fn, evac_fn, name, wbuf=None):
            with contextlib.ExitStack() as PS:
                if wbuf is None:
                    W = PS.enter_context(sb("W_" + name, [128, Kc * Mt * 128], BF16))
                else:
                    W = wbuf
                G0, Q0, M0, V0 = c["dg"], c["dq"], c["mm"], c["vv"]
                qc = Kc // 4
                wr = wd_ap.rearrange("(kc p) o -> p kc o", p=128)

                def fg(e):
                    e.dma_start(
                        W[:, 0:qc * Mt * 128].rearrange("p (kc o) -> p kc o", kc=qc),
                        wr[:, 0:qc, :]).then_inc(sem["dq"], 16)
                    e.dma_start(
                        W[:, qc * Mt * 128:Kc * Mt * 128].rearrange(
                            "p (kc o) -> p kc o", kc=Kc - qc),
                        wr[:, qc:Kc, :]).then_inc(sem["dg"], 16)

                def ft(e):
                    e.wait_ge(sem["dq"], Q0 + 16)
                    for mc in range(Mt):
                        for tn in range(2):
                            i = 2 * mc + tn
                            if i >= 4:
                                e.wait_ge(sem["vv"], V0 + i - 3)
                            for dc in range(Kc):
                                if i == 0 and dc == qc:
                                    e.wait_ge(sem["dg"], G0 + 16)
                                ins = e.matmul(
                                    pb[i % 4][:, :],
                                    W[:, (dc * Mt + mc) * 128:(dc * Mt + mc) * 128 + 128],
                                    rhs_fn(dc, tn),
                                    start=(dc == 0), stop=(dc == Kc - 1))
                            ins.then_inc(sem["mm"], 1)

                def fv(e):
                    for mc in range(Mt):
                        for tn in range(2):
                            i = 2 * mc + tn
                            e.wait_ge(sem["mm"], M0 + i + 1)
                            evac_fn(e, mc, tn, pb[i % 4][:, :])

                b.phase({"gpsimd": fg, "tensor": ft, "vector": fv})
                c["dg"] += 16
                c["dq"] += 16
                c["mm"] += 2 * Mt
                c["vv"] += 2 * Mt

        def copy_evac(dest_fn):
            def f(e, mc, tn, ps):
                e.tensor_copy(dest_fn(mc, tn), ps).then_inc(sem["vv"], 1)
            return f

        def emit_proj_split(wd_ap, Kc, Mt, nsplit, rhs_fn, evac_fn, name):
            Mh = Mt // nsplit
            for sp in range(nsplit):
                def ev(e, mc, tn, ps, sp=sp):
                    evac_fn(e, mc + Mh * sp, tn, ps)
                emit_proj(wd_ap[:, Mh * 128 * sp:Mh * 128 * (sp + 1)], Kc, Mh,
                          rhs_fn, ev, f"{name}_{sp}")

        def res_evac(e, mc, tn, ps):
            sl = slice(mc * NT + 512 * tn, mc * NT + 512 * tn + 512)
            e.tensor_add(XF[:, sl], XF[:, sl], ps).then_inc(sem["vv"], 1)

        def emit_layernorm():
            with contextlib.ExitStack() as PS:
                SC1 = PS.enter_context(sb("SC1", [128, 2048], BF16))
                SC1b = PS.enter_context(sb("SC1b", [128, 2048], BF16))
                M0, V0, A0 = c["mm"], c["vv"], c["aa"]

                def fv(e):
                    for cc in range(16):
                        if cc >= 2:
                            e.wait_ge(sem["mm"], M0 + cc - 1)
                        h = 1024 * (cc % 2)
                        e.tensor_copy(SC1[:, h:h + 1024],
                                      XF[:, cc * NT:(cc + 1) * NT]).then_inc(sem["vv"], 1)

                def fa(e):
                    for cc in range(16):
                        if cc >= 2:
                            e.wait_ge(sem["mm"], M0 + cc - 1)
                        h = 1024 * (cc % 2)
                        e.activation(SC1b[:, h:h + 1024],
                                     XF[:, cc * NT:(cc + 1) * NT],
                                     AF.Square).then_inc(sem["aa"], 1)

                def ft(e):
                    for cc in range(16):
                        e.wait_ge(sem["vv"], V0 + cc + 1)
                        e.wait_ge(sem["aa"], A0 + cc + 1)
                        h = 1024 * (cc % 2)
                        for tn in range(2):
                            e.matmul(pb[tn][0:1, :], INVD,
                                     SC1[:, h + 512 * tn:h + 512 * tn + 512],
                                     start=(cc == 0), stop=(cc == 15))
                        for tn in range(2):
                            ins = e.matmul(pb[2 + tn][0:1, :], INVD,
                                           SC1b[:, h + 512 * tn:h + 512 * tn + 512],
                                           start=(cc == 0), stop=(cc == 15))
                        ins.then_inc(sem["mm"], 1)

                b.phase({"vector": fv, "scalar": fa, "tensor": ft})
                c["mm"] += 16
                c["vv"] += 16
                c["aa"] += 16

                M1, V1, A1 = c["mm"], c["vv"], c["aa"]

                def fv2(e):
                    e.wait_ge(sem["mm"], M1)
                    ops = [
                        lambda: e.tensor_copy(SC2[0:1, 0:512], pb[0][0:1, :]),
                        lambda: e.tensor_copy(SC2[32:33, 0:512], pb[1][0:1, :]),
                        lambda: e.tensor_mul(SC3[32:33, 0:512], SC2[0:1, 0:512],
                                             SC2[0:1, 0:512]),
                        lambda: e.tensor_sub(SC2[64:65, 0:512], pb[2][0:1, :],
                                             SC3[32:33, 0:512]),
                        lambda: e.tensor_mul(SC3[32:33, 0:512], SC2[32:33, 0:512],
                                             SC2[32:33, 0:512]),
                        lambda: e.tensor_sub(SC3[0:1, 0:512], pb[3][0:1, :],
                                             SC3[32:33, 0:512]),
                        lambda: e.tensor_copy(SC1[0:1, 0:512], SC2[0:1, 0:512]),
                        lambda: e.tensor_copy(SC1[0:1, 512:1024], SC2[32:33, 0:512]),
                    ]
                    ins = None
                    for op in ops:
                        if ins is not None:
                            e.wait_ge(sem["sxv"], b.sx(ins))
                        ins = op()
                    ins.then_inc(sem["vv"], 1)

                def fa2(e):
                    e.wait_ge(sem["vv"], V1 + 1)
                    ops = [
                        lambda: e.activation(SC3[32:33, 0:512], SC2[64:65, 0:512],
                                             AF.Ln, bias=EPSV),
                        lambda: e.activation(SC1[0:1, 1024:1536], SC3[32:33, 0:512],
                                             AF.Exp, scale=-0.5),
                        lambda: e.activation(SC3[32:33, 0:512], SC3[0:1, 0:512],
                                             AF.Ln, bias=EPSV),
                        lambda: e.activation(SC1[0:1, 1536:2048], SC3[32:33, 0:512],
                                             AF.Exp, scale=-0.5),
                    ]
                    ins = None
                    for op in ops:
                        if ins is not None:
                            e.wait_ge(sem["sxa"], b.sx(ins, "sxa"))
                        ins = op()
                    ins.then_inc(sem["aa"], 1)

                b.phase({"vector": fv2, "scalar": fa2})
                c["vv"] += 1
                c["aa"] += 1

                M2, V2, A2 = c["mm"], c["vv"], c["aa"]

                def ft3(e):
                    e.wait_ge(sem["aa"], A2)
                    for tn in range(2):
                        e.matmul(pb[tn][:, :], ONESR, SC1[0:1, 512 * tn:512 * tn + 512],
                                 start=True, stop=True)
                        ins = e.matmul(pb[2 + tn][:, :], ONESR,
                                       SC1[0:1, 1024 + 512 * tn:1024 + 512 * tn + 512],
                                       start=True, stop=True)
                    ins.then_inc(sem["mm"], 1)

                def fv3(e):
                    e.wait_ge(sem["mm"], M2 + 1)
                    for cc in range(16):
                        for tn in range(2):
                            sl = slice(cc * NT + 512 * tn, cc * NT + 512 * tn + 512)
                            t = b.sx(e.tensor_sub(XF[:, sl], XF[:, sl], pb[tn][:, :]))
                            e.wait_ge(sem["sxv"], t)
                            t = b.sx(e.tensor_mul(XF[:, sl], XF[:, sl],
                                                  pb[2 + tn][:, :]))
                            e.wait_ge(sem["sxv"], t)
                    for cc in range(16):
                        ins = e.tensor_copy(XB[:, cc * NT:(cc + 1) * NT],
                                            XF[:, cc * NT:(cc + 1) * NT])
                    ins.then_inc(sem["vv"], 1)

                b.phase({"tensor": ft3, "vector": fv3})
                c["mm"] += 1
                c["vv"] += 1

        def emit_ag(layer, which, src_ap_fn):
            S0 = c["ds"]

            def fs(e):
                if which == "k":
                    e.dma_start(ag_kin[layer].ap().rearrange("(hh p) t -> p hh t", p=128),
                                src_ap_fn().rearrange("p (hh t) -> p hh t", hh=16)
                                ).then_inc(sem["ds"], 16)
                else:
                    e.dma_start(ag_vin[layer].ap().rearrange("(tt p) d2 -> p tt d2", p=128),
                                src_ap_fn().rearrange("p (tt d2) -> p tt d2", tt=8)
                                ).then_inc(sem["ds"], 16)

            def fg(e):
                e.wait_ge(sem["ds"], S0 + 16)
                src = ag_kin if which == "k" else ag_vin
                dst = ag_kout if which == "k" else ag_vout
                e.collective_compute("AllGather", ALU.bypass, GROUPS,
                                     [src[layer].ap().opt()],
                                     [dst[layer].ap().opt()]).then_inc(sem["cc"])

            b.phase({"sync": fs, "gpsimd": fg})
            c["ds"] += 16
            c["cc"] += 1

        def emit_rope(DST, TBL):
            M0, V0 = c["mm"], c["vv"]

            def ft(e):
                for i in range(32):
                    hc, tn = divmod(i, 2)
                    if i >= 2:
                        e.wait_ge(sem["vv"], V0 + i - 1)
                    ins = e.matmul(pb[i % 2][:, :], PMAT,
                                   DST[:, hc * NT + 512 * tn: hc * NT + 512 * tn + 512],
                                   start=True, stop=True)
                    ins.then_inc(sem["mm"], 1)

            def fv(e):
                for i in range(32):
                    hc, tn = divmod(i, 2)
                    sl = slice(hc * NT + 512 * tn, hc * NT + 512 * tn + 512)
                    e.wait_ge(sem["mm"], M0 + i + 1)
                    if i >= 1:
                        e.wait_ge(sem["vv"], V0 + i)
                    t = b.sx(e.tensor_mul(SC2[:, 0:512], DST[:, sl],
                                          TBL[:, 512 * tn:512 * tn + 512]))
                    e.wait_ge(sem["sxv"], t)
                    t = b.sx(e.tensor_mul(SC3[:, 0:512], pb[i % 2][:, :],
                                          TBL[:, 1024 + 512 * tn:1024 + 512 * tn + 512]))
                    e.wait_ge(sem["sxv"], t)
                    ins = e.tensor_add(DST[:, sl], SC2[:, 0:512], SC3[:, 0:512])
                    ins.then_inc(sem["vv"], 1)

            b.phase({"tensor": ft, "vector": fv})
            c["mm"] += 32
            c["vv"] += 32

        RAMPH = 6

        def emit_attention(layer, QTA, PTA, KVA, SPA, SPB, RAMP=None):
            kg = ag_kout[layer].ap()
            vg = ag_vout[layer].ap()
            nsg = NSG[layer]
            ntil = nsg + (8 if layer == 0 else 0)
            span = 2 * ntil * 128
            ns0, ns1 = NS[layer]

            def KTo(ph):
                return ph * span

            def VTo(ph):
                return ph * span + ntil * 128

            def slot_cols(sig, i):
                nsp = (ns0 if sig == 0 else ns1)
                ng = nsp - (4 if layer == 0 else 0)
                if layer == 0 and i >= ng:
                    return (nsg + 4 * sig + (i - ng)) * 128
                return i * 128

            S0, M0, V0, A0, C0 = c["ds"], c["mm"], c["vv"], c["aa"], c["cc"]
            epp = 1 if layer == 0 else 2
            mm_h = (ns0 + 1) + (ns1 + 1)
            aa_h = (ns0 // epp + 1) + (ns1 // epp + 1)
            nmask = 4 if layer == 0 else 0
            PBK = nmask + 3
            vv_h = 2 * PBK

            def mm_base(h, sig):
                return M0 + h * mm_h + (ns0 + 1 if sig else 0)

            def aa_base(h, sig):
                return A0 + h * aa_h + (ns0 // epp + 1 if sig else 0)

            def vv_base(h, sig):
                return V0 + h * vv_h + (PBK if sig else 0)

            nstripe = 7 if layer == 0 else 8
            ndma_h = 2 * nstripe + (2 if layer == 0 else 0)
            G0r = c["dg"]

            def fgr(e):
                e.dma_start(RAMP[:], rampt.ap()).then_inc(sem["dg"], 16)

            def fs(e):
                e.wait_ge(sem["cc"], C0)
                for h in range(H):
                    ph = h % 2
                    if h >= 2:
                        e.wait_ge(sem["mm"], mm_base(h - 1, 0))
                    kga = kg.rearrange("(r hh p) (cb t) -> r hh p cb t",
                                       r=4, hh=H, p=128, cb=8, t=128)
                    vga = vg.rearrange("(r cb p) d2 -> r cb p d2",
                                       r=4, cb=8, p=128)
                    for s in range(nstripe):
                        r = s if s < 4 else 7 - s
                        cb0 = 0 if s < 4 else 4
                        e.dma_start(
                            KVA[:, KTo(ph) + 512 * s:KTo(ph) + 512 * s + 512].rearrange(
                                "p (cb t) -> p cb t", cb=4, t=128),
                            kga[r, h][:, cb0:cb0 + 4, :]).then_inc(sem["ds"], 16)
                        e.dma_start(
                            KVA[:, VTo(ph) + 512 * s:VTo(ph) + 512 * s + 512].rearrange(
                                "p (cb d2) -> p cb d2", cb=4, d2=128),
                            vga[r, cb0:cb0 + 4, :, 128 * h:128 * h + 128].rearrange(
                                "cb p d2 -> p cb d2")).then_inc(sem["ds"], 16)
                    if layer == 0:
                        e.dma_start(
                            KVA[:, KTo(ph) + nsg * 128:KTo(ph) + nsg * 128 + 1024],
                            ag_kin[0].ap().rearrange("(hh p) t -> hh p t",
                                                     hh=H, p=128)[h]
                        ).then_inc(sem["ds"], 16)
                        e.dma_start(
                            KVA[:, VTo(ph) + nsg * 128:VTo(ph) + nsg * 128 + 1024].rearrange(
                                "p (cb d2) -> p cb d2", cb=8, d2=128),
                            ag_vin[0].ap().rearrange("(cb p) d2 -> p cb d2",
                                                     cb=8, p=128)
                            [:, :, 128 * h:128 * h + 128]).then_inc(sem["ds"], 16)

            R0 = c["mr"]

            def ft(e):
                pending = []

                def flush_repl():
                    while pending:
                        aaw, vvw, nrep = pending.pop(0)
                        e.wait_ge(sem["aa"], aaw)
                        e.wait_ge(sem["vv"], vvw)
                        ins = e.matmul(pb[3][:, :], ONESR, RCPB[0:1, :],
                                       start=True, stop=True)
                        ins.then_inc(sem["mr"], 1)

                if layer == 0:
                    e.wait_ge(sem["dg"], G0r + 16)
                for h in range(H):
                    ph = h % 2
                    e.wait_ge(sem["ds"], S0 + 16 * ndma_h * (h + 1))
                    for sig in range(2):
                        blk = 2 * h + sig
                        nsp = ns0 if sig == 0 else ns1
                        mmb = mm_base(h, sig)
                        aab = aa_base(h, sig)
                        vvb = vv_base(h, sig)
                        obank = pb[0] if blk % 2 == 0 else pb[1]
                        qr = QTA[:, h * NT + 512 * sig: h * NT + 512 * sig + 512]

                        def sap(i):
                            SP = SPA if (i // 2) % 2 == 0 else SPB
                            return SP[:, 512 * (i % 2):512 * (i % 2) + 512]
                        use_ramp = (layer == 0 and h < RAMPH)
                        if use_ramp:
                            rp = 32 * (h % 3)
                            rb = 1024 * (h // 3)
                            rhi = RAMP[rp:rp + 1, rb:rb + 512]
                            rlo = RAMP[rp:rp + 1, rb + 512:rb + 1024]

                        def s_chunk(i):
                            if i >= 4:
                                e.wait_ge(sem["aa"], aab + (i - 4) // epp + 1)
                            elif aab > A0:
                                e.wait_ge(sem["aa"], aab - 1)
                            sb_ = sap(i)
                            ins = e.matmul(sb_,
                                           KVA[:, KTo(ph) + slot_cols(sig, i):
                                               KTo(ph) + slot_cols(sig, i) + 128],
                                           qr, start=True, stop=not use_ramp)
                            if use_ramp:
                                e.matmul(sb_, ONESM[rp:rp + 1, :], rhi,
                                         start=False, stop=False)
                                ins = e.matmul(sb_, ONESM[rp:rp + 1, :],
                                               rlo, start=False, stop=True)
                            ins.then_inc(sem["mm"], 1)

                        def pvpair(i0):
                            e.wait_ge(sem["aa"], aab + (i0 + 1) // epp + 1)
                            if i0 == 0:
                                e.wait_ge(sem["vv"], vvb)
                            for j in (i0, i0 + 1):
                                pt = PTA[:, 512 * (j % 4):512 * (j % 4) + 512]
                                if j % 2 == 0:
                                    e.matmul(pb[2][0:1, :], ONESC, pt,
                                             start=(j == 0), stop=(j == nsp - 2),
                                             tile_position=(0, 0))
                                else:
                                    e.matmul(pb[3][64:65, :], ONESC, pt,
                                             start=(j == 1), stop=(j == nsp - 1),
                                             tile_position=(0, 64))
                            for j in (i0, i0 + 1):
                                pt = PTA[:, 512 * (j % 4):512 * (j % 4) + 512]
                                ins = e.matmul(obank[:, :],
                                               KVA[:, VTo(ph) + slot_cols(sig, j):
                                                   VTo(ph) + slot_cols(sig, j) + 128],
                                               pt, start=(j == 0),
                                               stop=(j == nsp - 1))
                            if i0 + 1 == nsp - 1:
                                ins.then_inc(sem["mm"], 1)

                        for i in range(nsp):
                            s_chunk(i)
                            if i == 3:
                                flush_repl()
                            if i % 2 == 1 and i >= 3:
                                pvpair(i - 3)
                        pvpair(nsp - 2)
                        pending.append((aab + nsp // epp + 1, vvb, 0))
                if pending:
                    flush_repl()

            def fa(e):
                for h in range(H):
                    for sig in range(2):
                        nsp = ns0 if sig == 0 else ns1
                        mmb = mm_base(h, sig)
                        vvb = vv_base(h, sig)

                        def sapf(i):
                            SP = SPA if (i // 2) % 2 == 0 else SPB
                            return SP[:, 512 * (i % 2):512 * (i % 2) + 512]

                        if layer == 0:
                            for i in range(nsp):
                                e.wait_ge(sem["mm"], mmb + i + 1)
                                if i >= nsp - 4:
                                    e.wait_ge(sem["vv"], vvb + (i - (nsp - 4)) + 1)
                                bias_ap = ALB[:, h * 48 + (0 if sig == 0 else 16) + i:
                                              h * 48 + (0 if sig == 0 else 16) + i + 1]
                                ins = e.activation(PTA[:, 512 * (i % 4):512 * (i % 4) + 512],
                                                   sapf(i), AF.Exp,
                                                   bias=bias_ap, scale=SCALE)
                                ins.then_inc(sem["aa"], 1)
                        else:
                            for j in range(nsp // 2):
                                e.wait_ge(sem["mm"], mmb + 2 * j + 2)
                                SP = SPA if j % 2 == 0 else SPB
                                ins = e.activation(PTA[:, 1024 * (j % 2):1024 * (j % 2) + 1024],
                                                   SP[:, :], AF.Exp)
                                ins.then_inc(sem["aa"], 1)
                        e.wait_ge(sem["vv"], vvb + nmask + 1)
                        t = b.sx(e.activation(SC3[64:65, 0:512], SC2[32:33, 0:512],
                                              AF.Ln), "sxa")
                        e.wait_ge(sem["sxa"], t)
                        ins = e.activation(RCPB[0:1, :], SC3[64:65, 0:512],
                                           AF.Exp, scale=-1.0)
                        ins.then_inc(sem["aa"], 1)

            def fv(e):
                for h in range(H):
                    for sig in range(2):
                        blk = 2 * h + sig
                        nsp = ns0 if sig == 0 else ns1
                        mmb = mm_base(h, sig)
                        aab = aa_base(h, sig)
                        obank = pb[0] if blk % 2 == 0 else pb[1]
                        if layer == 0:
                            for ii in range(4):
                                i = nsp - 4 + ii
                                e.wait_ge(sem["mm"], mmb + i + 1)
                                SP = SPA if (i // 2) % 2 == 0 else SPB
                                ins = e.copy_predicated(
                                    SP[:, 512 * (i % 2):512 * (i % 2) + 512],
                                    MASK[ii], NEGC[:, :])
                                ins.then_inc(sem["vv"], 1)
                        e.wait_ge(sem["mm"], mmb + nsp + 1)
                        t = b.sx(e.tensor_copy(SC2[0:1, 0:512], pb[3][64:65, :]))
                        e.wait_ge(sem["sxv"], t)
                        ins = e.tensor_add(SC2[32:33, 0:512], pb[2][0:1, :],
                                           SC2[0:1, 0:512])
                        ins.then_inc(sem["vv"], 1)
                        e.wait_ge(sem["mr"], R0 + blk + 1)
                        ins = e.tensor_copy(SC3[:, 0:512], pb[3][:, :])
                        t = b.sx(ins)
                        e.wait_ge(sem["sxv"], t)
                        ins = e.tensor_mul(
                            XB[:, h * NT + 512 * sig: h * NT + 512 * sig + 512],
                            obank[:, :], SC3[:, 0:512])
                        ins.then_inc(sem["vv"], 2)

            fns = {"sync": fs, "tensor": ft, "scalar": fa, "vector": fv}
            if layer == 0:
                fns["gpsimd"] = fgr
            b.phase(fns)
            if layer == 0:
                c["dg"] += 16
            c["ds"] += 16 * ndma_h * H
            c["mm"] += mm_h * H
            c["aa"] += aa_h * H
            c["vv"] += vv_h * H
            c["mr"] += 2 * H

        def dump(name, src_ap, dtype, cols):
            if name not in dbg_list:
                return
            od = dt("dbg_" + name, [128, cols], dtype, kind="ExternalOutput")
            dbg_out[name] = od
            S0 = c["ds"]

            def fs(e):
                e.dma_start(od.ap(), src_ap).then_inc(sem["ds"], 16)
                e.wait_ge(sem["ds"], S0 + 16)

            b.phase({"sync": fs})
            c["ds"] += 16

        with contextlib.ExitStack() as S1:
            CT = S1.enter_context(sb("CT", [128, 4 * NT], BF16))
            ARENA = S1.enter_context(sb("ARENA", [128, 16 * NT], BF16))

            emit_proj(wkvdT.ap(), 16, 4, xb_rhs,
                      copy_evac(lambda mc, tn: CT[:, mc * NT + 512 * tn:
                                                  mc * NT + 512 * tn + 512]), "kvd")
            dump("ct", CT[:, 0:4 * NT], BF16, 4 * NT)

            with contextlib.ExitStack() as S1k:
                KUP = S1k.enter_context(sb("KUP", [128, H * 4 * 128], BF16))
                M0, V0, G0 = c["mm"], c["vv"], c["dg"]

                def ftk(e):
                    e.wait_ge(sem["dg"], G0 + 16)
                    for h in range(H):
                        for tn in range(2):
                            i = 2 * h + tn
                            if i >= 4:
                                e.wait_ge(sem["vv"], V0 + i - 3)
                            for rc in range(4):
                                ins = e.matmul(
                                    pb[i % 4][:, :],
                                    KUP[:, (h * 4 + rc) * 128:(h * 4 + rc) * 128 + 128],
                                    CT[:, rc * NT + 512 * tn:rc * NT + 512 * tn + 512],
                                    start=(rc == 0), stop=(rc == 3))
                            ins.then_inc(sem["mm"], 1)

                def fvk(e):
                    for i in range(32):
                        h, tn = divmod(i, 2)
                        e.wait_ge(sem["mm"], M0 + i + 1)
                        e.tensor_copy(ARENA[:, h * NT + 512 * tn:h * NT + 512 * tn + 512],
                                      pb[i % 4][:, :]).then_inc(sem["vv"], 1)

                def fgk(e):
                    e.dma_start(KUP[:].rearrange("p (hh r dd) -> p hh r dd", hh=H, r=4),
                                kupw.ap().rearrange("hh (r p) dd -> p hh r dd", p=128)
                                ).then_inc(sem["dg"], 16)

                b.phase({"gpsimd": fgk, "tensor": ftk, "vector": fvk})
                c["dg"] += 16
                c["mm"] += 32
                c["vv"] += 32
            dump("k0", ARENA[:, 0:16 * NT], BF16, 16 * NT)
            emit_ag(0, "k", lambda: ARENA[:])

            with contextlib.ExitStack() as S1v:
                VUP = S1v.enter_context(sb("VUP", [128, 4 * D], BF16))
                M0, V0, G0 = c["mm"], c["vv"], c["dg"]

                def ftv(e):
                    e.wait_ge(sem["dg"], G0 + 16)
                    for i in range(32):
                        tt, oc = divmod(i, 4)
                        if i >= 4:
                            e.wait_ge(sem["vv"], V0 + i - 3)
                        for rc in range(4):
                            ins = e.matmul(
                                pb[i % 4][:, :],
                                CT[:, rc * NT + 128 * tt:rc * NT + 128 * tt + 128],
                                VUP[:, rc * D + 512 * oc:rc * D + 512 * oc + 512],
                                start=(rc == 0), stop=(rc == 3))
                        ins.then_inc(sem["mm"], 1)

                def fvv(e):
                    for i in range(32):
                        tt, oc = divmod(i, 4)
                        e.wait_ge(sem["mm"], M0 + i + 1)
                        e.tensor_copy(ARENA[:, tt * D + 512 * oc:tt * D + 512 * oc + 512],
                                      pb[i % 4][:, :]).then_inc(sem["vv"], 1)

                def fgv0(e):
                    e.dma_start(VUP[:].rearrange("p (r d2) -> p r d2", r=4),
                                vupflat.ap().rearrange("(r p) d2 -> p r d2", p=128)
                                ).then_inc(sem["dg"], 16)

                b.phase({"gpsimd": fgv0, "tensor": ftv, "vector": fvv})
                c["dg"] += 16
                c["mm"] += 32
                c["vv"] += 32
            dump("v0", ARENA[:, 0:8 * D], BF16, 8 * D)
            emit_ag(0, "v", lambda: ARENA[:, 0:8 * D])

        QTS = contextlib.ExitStack()
        QTA = QTS.enter_context(sb("QTA", [128, 16 * NT], BF16))
        with contextlib.ExitStack() as S2:
            QD = S2.enter_context(sb("QD", [128, 12 * NT], BF16))
            emit_proj_split(wqdT.ap(), 16, 12, 2, xb_rhs,
                            copy_evac(lambda mc, tn: QD[:, mc * NT + 512 * tn:
                                                        mc * NT + 512 * tn + 512]), "qd")

            def qd_rhs(rc, tn):
                return QD[:, rc * NT + 512 * tn: rc * NT + 512 * tn + 512]

            for mh in range(2):
                emit_proj(wqupT.ap()[:, 1024 * mh:1024 * (mh + 1)], 12, 8, qd_rhs,
                          copy_evac(lambda mc, tn, mh=mh:
                                    QTA[:, (mc + 8 * mh) * NT + 512 * tn:
                                        (mc + 8 * mh) * NT + 512 * tn + 512]),
                          f"qup{mh}")
        dump("q0", QTA[:, 0:16 * NT], BF16, 16 * NT)

        with contextlib.ExitStack() as S3:
            PTA = S3.enter_context(sb("PTA1", [128, 4 * 512], BF16))
            KVA = S3.enter_context(sb("KVA1", [128, 2 * 2 * 36 * 128], BF16))
            RAMP = S3.enter_context(sb("RAMP", [128, 2048], BF16))
            SPA = S3.enter_context(nc.psum_tensor("spa0", [128, 1024], F32))
            SPB = S3.enter_context(nc.psum_tensor("spb0", [128, 1024], F32))
            emit_attention(0, QTA, PTA, KVA, SPA, SPB, RAMP)
        QTS.close()
        dump("ao0", XB[:, 0:16 * NT], BF16, 16 * NT)

        emit_proj_split(woT0.ap(), 16, 16, 2, xb_rhs, res_evac, "wo0")

        with contextlib.ExitStack() as S5:
            FUSE = S5.enter_context(sb("FUSE", [128, 64 * 512], BF16))
            WG = S5.enter_context(sb("WG", [128, 2 * 2048], BF16))
            WU = S5.enter_context(sb("WU", [128, 2 * 2048], BF16))
            SGF = S5.enter_context(sb("SGF", [128, 128], F32))
            pd = [S5.enter_context(nc.psum_tensor(f"pd{i}", [128, 512], F32))
                  for i in range(4)]

            def fld(e):
                e.dma_start(SGF[:, 0:64], sgate.ap()).then_inc(sem["dg"], 16)
                e.dma_start(SGF[:, 64:128], supv.ap()).then_inc(sem["dg"], 16)
                for oc in range(2):
                    par = oc % 2
                    e.dma_start(
                        WG[:, par * 2048:(par + 1) * 2048].rearrange(
                            "p (kc o) -> p kc o", kc=16),
                        wg8.ap()[oc]).then_inc(sem["dg"], 16)
                    e.dma_start(
                        WU[:, par * 2048:(par + 1) * 2048].rearrange(
                            "p (kc o) -> p kc o", kc=16),
                        wu8.ap()[oc]).then_inc(sem["dg"], 16)

            b.phase({"gpsimd": fld})
            c["dg"] += 32
            Gpre = c["dg"]
            c["dg"] += 64
            emit_layernorm()
            dump("x1", XF[:, 0:16 * NT], F32, 16 * NT)
            WD = S5.enter_context(sb("WD", [128, 2 * 4096], BF16))

            for th in range(2):
                G0, M0, V0, A0 = (Gpre if th == 0 else c["dg"]), c["mm"], c["vv"], c["aa"]

                def fg(e, th=th, M0=M0):
                    for oc in range(2 if th == 0 else 0, 64):
                        if oc >= 2:
                            e.wait_ge(sem["mm"], M0 + 2 * (oc - 1))
                        par = oc % 2
                        e.dma_start(
                            WG[:, par * 2048:(par + 1) * 2048].rearrange(
                                "p (kc o) -> p kc o", kc=16),
                            wg8.ap()[oc]).then_inc(sem["dg"], 16)
                        e.dma_start(
                            WU[:, par * 2048:(par + 1) * 2048].rearrange(
                                "p (kc o) -> p kc o", kc=16),
                            wu8.ap()[oc]).then_inc(sem["dg"], 16)

                def ft(e, th=th, G0=G0, V0=V0, A0=A0):
                    for oc in range(64):
                        par = oc % 2
                        e.wait_ge(sem["dg"], G0 + 32 * (oc + 1))
                        if oc >= 2:
                            e.wait_ge(sem["aa"], A0 + oc - 1)
                            e.wait_ge(sem["vv"], V0 + oc - 1)
                        for dc in range(16):
                            e.matmul(pb[2 * par][:, :],
                                     WG[:, par * 2048 + dc * 128:par * 2048 + dc * 128 + 128],
                                     XB[:, dc * NT + 512 * th:dc * NT + 512 * th + 512],
                                     start=(dc == 0), stop=(dc == 15))
                        for dc in range(16):
                            ins = e.matmul(pb[2 * par + 1][:, :],
                                           WU[:, par * 2048 + dc * 128:par * 2048 + dc * 128 + 128],
                                           XB[:, dc * NT + 512 * th:dc * NT + 512 * th + 512],
                                           start=(dc == 0), stop=(dc == 15))
                        ins.then_inc(sem["mm"], 2)

                def fa(e, th=th, M0=M0, V0=V0):
                    for oc in range(64):
                        par = oc % 2
                        e.wait_ge(sem["mm"], M0 + 2 * (oc + 1))
                        if oc >= 1:
                            e.wait_ge(sem["vv"], V0 + oc)
                        ins = e.activation(SC3[:, 0:512], pb[2 * par][:, :],
                                           AF.Silu, scale=SGF[:, oc:oc + 1])
                        ins.then_inc(sem["aa"], 1)

                def fv(e, th=th, A0=A0):
                    for oc in range(64):
                        par = oc % 2
                        e.wait_ge(sem["aa"], A0 + oc + 1)
                        ins = e.scalar_tensor_tensor(
                            FUSE[:, 512 * oc:512 * oc + 512],
                            pb[2 * par + 1][:, :], SGF[:, 64 + oc:64 + oc + 1],
                            SC3[:, 0:512], ALU.mult, ALU.mult)
                        ins.then_inc(sem["vv"], 1)

                b.phase({"gpsimd": fg, "tensor": ft, "scalar": fa, "vector": fv})
                c["dg"] += 64 * 32 - (64 if th == 0 else 0)
                c["mm"] += 128
                c["aa"] += 64
                c["vv"] += 64

                for half in range(2):
                    G0, M0, V0 = c["dg"], c["mm"], c["vv"]

                    def fgd(e, G0=G0, M0=M0):
                        for oc in range(64):
                            if oc >= 2:
                                e.wait_ge(sem["mm"], M0 + oc - 1)
                            par = oc % 2
                            e.dma_start(WD[:, par * 2048:(par + 1) * 2048],
                                        wd8.ap()[128 * oc:128 * (oc + 1), :]
                                        ).then_inc(sem["dg"], 16)

                    def ftd(e, half=half, G0=G0):
                        banks = pb[0:4] + pd[0:4]
                        for oc in range(64):
                            par = oc % 2
                            e.wait_ge(sem["dg"], G0 + 16 * (oc + 1))
                            for k8 in range(8):
                                mc = 8 * half + k8
                                ins = e.matmul(
                                    banks[k8][:, :],
                                    WD[:, par * 2048 + mc * 128:par * 2048 + mc * 128 + 128],
                                    FUSE[:, 512 * oc:512 * oc + 512],
                                    start=(oc == 0), stop=(oc == 63))
                            ins.then_inc(sem["mm"], 1)

                    def fvd(e, th=th, half=half, M0=M0):
                        e.wait_ge(sem["mm"], M0 + 64)
                        banks = pb[0:4] + pd[0:4]
                        for k8 in range(8):
                            mc = 8 * half + k8
                            sl = slice(mc * NT + 512 * th, mc * NT + 512 * th + 512)
                            ins = e.scalar_tensor_tensor(
                                XF[:, sl], banks[k8][:, :], SDN[:, mc:mc + 1],
                                XF[:, sl], ALU.mult, ALU.add)
                            ins.then_inc(sem["vv"], 1)

                    b.phase({"gpsimd": fgd, "tensor": ftd, "vector": fvd})
                    c["dg"] += 64 * 16
                    c["mm"] += 64
                    c["vv"] += 8

        emit_layernorm()
        dump("x2", XF[:, 0:16 * NT], F32, 16 * NT)

        with contextlib.ExitStack() as S6:
            PTA2 = S6.enter_context(sb("PTA2", [128, 16 * NT], BF16))
            RKT = S6.enter_context(sb("RKT", [128, 2 * NT], BF16))
            S0k = c["ds"]

            def fldk(e):
                e.dma_start(RKT[:], ropek.ap()).then_inc(sem["ds"], 16)
                e.wait_ge(sem["ds"], S0k + 16)

            b.phase({"sync": fldk})
            c["ds"] += 16
            emit_proj_split(wk1.ap(), 16, 16, 2, xb_rhs,
                            copy_evac(lambda mc, tn: PTA2[:, mc * NT + 512 * tn:
                                                          mc * NT + 512 * tn + 512]), "wk1")
            emit_rope(PTA2, RKT)
            dump("k1", PTA2[:, 0:16 * NT], BF16, 16 * NT)
            emit_ag(1, "k", lambda: PTA2[:])

        with contextlib.ExitStack() as S7:
            KVA2 = S7.enter_context(sb("KVA2", [128, 8 * D], BF16))
            for vh in range(2):
                with contextlib.ExitStack() as S7b:
                    WV = S7b.enter_context(sb("WV", [128, 16 * NT], BF16))
                    G0, M0, V0 = c["dg"], c["mm"], c["vv"]

                    def fgv(e, vh=vh):
                        e.dma_start(WV[:].rearrange("p (kc o) -> p kc o", kc=16),
                                    wv1.ap()[:, NT * vh:NT * (vh + 1)].rearrange(
                                        "(kc p) o -> p kc o", p=128)
                                    ).then_inc(sem["dg"], 16)

                    def ftv1(e, vh=vh, G0=G0, V0=V0):
                        e.wait_ge(sem["dg"], G0 + 16)
                        for i in range(16):
                            tt, oc = divmod(i, 2)
                            if i >= 4:
                                e.wait_ge(sem["vv"], V0 + i - 3)
                            for dc in range(16):
                                ins = e.matmul(
                                    pb[i % 4][:, :],
                                    XB[:, dc * NT + 128 * tt:dc * NT + 128 * tt + 128],
                                    WV[:, dc * NT + 512 * oc:dc * NT + 512 * oc + 512],
                                    start=(dc == 0), stop=(dc == 15))
                            ins.then_inc(sem["mm"], 1)

                    def fvv1(e, vh=vh, M0=M0):
                        for i in range(16):
                            tt, oc = divmod(i, 2)
                            e.wait_ge(sem["mm"], M0 + i + 1)
                            e.tensor_copy(
                                KVA2[:, tt * D + NT * vh + 512 * oc:
                                     tt * D + NT * vh + 512 * oc + 512],
                                pb[i % 4][:, :]).then_inc(sem["vv"], 1)

                    b.phase({"gpsimd": fgv, "tensor": ftv1, "vector": fvv1})
                    c["dg"] += 16
                    c["mm"] += 16
                    c["vv"] += 16
            dump("v1", KVA2[:, 0:8 * D], BF16, 8 * D)
            emit_ag(1, "v", lambda: KVA2[:])

        QTS = contextlib.ExitStack()
        QTA = QTS.enter_context(sb("QTA2", [128, 16 * NT], BF16))
        with contextlib.ExitStack() as S8:
            RQT = S8.enter_context(sb("RQT", [128, 2 * NT], BF16))
            S0q = c["ds"]

            def fldq(e):
                e.dma_start(RQT[:], ropeq.ap()).then_inc(sem["ds"], 16)
                e.wait_ge(sem["ds"], S0q + 16)

            b.phase({"sync": fldq})
            c["ds"] += 16
            emit_proj_split(wq1.ap(), 16, 16, 2, xb_rhs,
                            copy_evac(lambda mc, tn: QTA[:, mc * NT + 512 * tn:
                                                         mc * NT + 512 * tn + 512]), "wq1")
            emit_rope(QTA, RQT)
        dump("q1", QTA[:, 0:16 * NT], BF16, 16 * NT)

        with contextlib.ExitStack() as S9:
            PTA = S9.enter_context(sb("PTA3", [128, 4 * 512], BF16))
            KVA = S9.enter_context(sb("KVA3", [128, 2 * 2 * 32 * 128], BF16))
            SPA = S9.enter_context(nc.psum_tensor("spa1", [128, 1024], F32))
            SPB = S9.enter_context(nc.psum_tensor("spb1", [128, 1024], F32))
            emit_attention(1, QTA, PTA, KVA, SPA, SPB)
        QTS.close()
        dump("ao1", XB[:, 0:16 * NT], BF16, 16 * NT)

        emit_proj_split(wo1.ap(), 16, 16, 2, xb_rhs, res_evac, "wo1")
        emit_layernorm()
        dump("x3", XF[:, 0:16 * NT], F32, 16 * NT)

        with contextlib.ExitStack() as SA:
            ATS = SA.enter_context(sb("ATS", [128, 16 * 128], BF16))
            BST = SA.enter_context(sb("BST", [128, 16 * 128], BF16))
            GWT = SA.enter_context(sb("GWT", [128, 16 * 8], BF16))
            GBR = SA.enter_context(sb("GBR", [128, 8], F32))
            WT = SA.enter_context(sb("WTb", [128, 2 * 16 * 128], BF16))
            G1 = SA.enter_context(sb("G1", [128, 64], BF16))
            GS = SA.enter_context(sb("GS", [128, 8], F32))
            XAW = SA.enter_context(sb("XAW", [128, NT], BF16))
            WRB = SA.enter_context(sb("WRB", [128, NT], BF16))
            pbh = SA.enter_context(nc.psum_tensor("pbh", [128, 1024], BF16))
            PXL = SA.enter_context(nc.psum_tensor("pxl", [128, 512], F32))
            PXW = SA.enter_context(nc.psum_tensor("pxw", [128, 512], F32))

            G0 = c["dg"]

            def fl1(e):
                e.dma_start(ATS[:].rearrange("p (kc o) -> p kc o", kc=16),
                            aTw.ap().rearrange("(kc p) o -> p kc o", p=128)
                            ).then_inc(sem["dg"], 16)
                e.dma_start(BST[:].rearrange("p (kc o) -> p kc o", kc=1),
                            bstackT.ap().rearrange("(kc p) o -> p kc o", p=128)
                            ).then_inc(sem["dg"], 16)
                e.dma_start(GWT[:].rearrange("p (kc o) -> p kc o", kc=16),
                            gwT.ap().rearrange("(kc p) o -> p kc o", p=128)
                            ).then_inc(sem["dg"], 16)
                e.dma_start(GBR[:], gbrep.ap()).then_inc(sem["dg"], 16)
                e.wait_ge(sem["dg"], G0 + 64)

            b.phase({"gpsimd": fl1})
            c["dg"] += 64

            M0, V0, A0 = c["mm"], c["vv"], c["aa"]

            def ftg(e):
                for tt in range(8):
                    for dc in range(16):
                        ins = e.matmul(PXL[:, 8 * tt:8 * tt + 8],
                                       XB[:, dc * NT + 128 * tt:dc * NT + 128 * tt + 128],
                                       GWT[:, dc * 8:dc * 8 + 8],
                                       start=(dc == 0), stop=(dc == 15))
                ins.then_inc(sem["mm"], 1)
                e.wait_ge(sem["vv"], V0 + 9)
                for tt in range(8):
                    ins = e.transpose(pbh[0:8, 128 * tt:128 * tt + 128],
                                      G1[:, 8 * tt:8 * tt + 8], IDN)
                ins.then_inc(sem["mm"], 1)
                for tn in range(2):
                    e.wait_ge(sem["vv"], V0 + 10 + tn)
                    ins = e.matmul(PXW[:, :], E8[:, 0:128],
                                   E8[:, 128 + 512 * tn:128 + 512 * tn + 512],
                                   start=True, stop=True)
                    ins.then_inc(sem["mm"], 1)
                for tn in range(2):
                    for dc in range(16):
                        ins = e.matmul(pb[tn][:, :],
                                       ATS[:, dc * 128:dc * 128 + 128],
                                       XB[:, dc * NT + 512 * tn:dc * NT + 512 * tn + 512],
                                       start=(dc == 0), stop=(dc == 15))
                    ins.then_inc(sem["mm"], 1)

            def fvg(e):
                def ch(ins):
                    e.wait_ge(sem["sxv"], b.sx(ins))
                    return ins

                e.wait_ge(sem["mm"], M0 + 1)
                for tt in range(8):
                    lg = SC2[:, 32 * tt:32 * tt + 8]
                    ch(e.tensor_add(lg, PXL[:, 8 * tt:8 * tt + 8], GBR[:, 0:8]))
                    ch(e.reduce_max(GS[:, 0:1], lg, axis=AX.X))
                    ch(e.tensor_scalar_mul(GS[:, 1:2], GS[:, 0:1], -1.0))
                    ch(e.tensor_scalar(SC2[:, 32 * tt + 8:32 * tt + 16], lg, GS[:, 0:1],
                                       None, ALU.is_equal))
                    ch(e.scalar_tensor_tensor(SC2[:, 32 * tt + 8:32 * tt + 16],
                                              SC2[:, 32 * tt + 8:32 * tt + 16], KILL,
                                              lg, ALU.mult, ALU.add))
                    ins = e.reduce_max(GS[:, 2:3], SC2[:, 32 * tt + 8:32 * tt + 16],
                                       axis=AX.X)
                    ins.then_inc(sem["vv"], 1)
                    e.wait_ge(sem["aa"], A0 + tt + 1)
                    ex = SC2[:, 32 * tt + 16:32 * tt + 24]
                    ch(e.tensor_scalar(SC2[:, 32 * tt + 24:32 * tt + 32], lg, GS[:, 2:3],
                                       None, ALU.is_ge))
                    ch(e.tensor_mul(ex, ex, SC2[:, 32 * tt + 24:32 * tt + 32]))
                    ch(e.reduce_sum(GS[:, 3:4], ex, axis=AX.X))
                    ch(e.reciprocal(GS[:, 4:5], GS[:, 3:4]))
                    ins = e.tensor_scalar_mul(G1[:, 8 * tt:8 * tt + 8], ex,
                                              GS[:, 4:5])
                    if tt == 7:
                        ins.then_inc(sem["vv"], 1)
                e.wait_ge(sem["mm"], M0 + 2)
                e.tensor_copy(E8[:, 128:128 + 1024], pbh[0:8, :]).then_inc(sem["vv"], 1)
                for tn in range(2):
                    e.wait_ge(sem["mm"], M0 + 3 + tn)
                    e.tensor_copy(WRB[:, 512 * tn:512 * tn + 512],
                                  PXW[:, :]).then_inc(sem["vv"], 1)
                for tn in range(2):
                    e.wait_ge(sem["mm"], M0 + 5 + tn)
                    e.tensor_mul(XAW[:, 512 * tn:512 * tn + 512], pb[tn][:, :],
                                 WRB[:, 512 * tn:512 * tn + 512]).then_inc(sem["vv"], 1)

            def fag(e):
                for tt in range(8):
                    e.wait_ge(sem["vv"], V0 + tt + 1)
                    lg = SC2[:, 32 * tt:32 * tt + 8]
                    ins = e.activation(SC2[:, 32 * tt + 16:32 * tt + 24], lg,
                                       AF.Exp, bias=GS[:, 1:2])
                    ins.then_inc(sem["aa"], 1)

            b.phase({"tensor": ftg, "vector": fvg, "scalar": fag})
            c["mm"] += 6
            c["vv"] += 14
            c["aa"] += 8

            G0, M0, V0 = c["dg"], c["mm"], c["vv"]

            def fgb(e):
                for oc in range(16):
                    if oc >= 2:
                        e.wait_ge(sem["mm"], M0 + 2 * (oc - 1))
                    par = oc % 2
                    e.dma_start(
                        WT[:, par * 2048:(par + 1) * 2048].rearrange(
                            "p (kc o) -> p kc o", kc=16),
                        wT1.ap()[:, 128 * oc:128 * (oc + 1)].rearrange(
                            "(kc p) o -> p kc o", p=128)).then_inc(sem["dg"], 16)

            def ftb(e):
                for oc in range(16):
                    par = oc % 2
                    e.wait_ge(sem["dg"], G0 + 16 * (oc + 1))
                    if oc >= 2:
                        e.wait_ge(sem["vv"], V0 + 2 * (oc - 1))
                    for tn in range(2):
                        for dc in range(16):
                            e.matmul(pb[2 * par + tn][:, :],
                                     WT[:, par * 2048 + dc * 128:par * 2048 + dc * 128 + 128],
                                     XB[:, dc * NT + 512 * tn:dc * NT + 512 * tn + 512],
                                     start=(dc == 0), stop=False)
                        ins = e.matmul(pb[2 * par + tn][:, :],
                                       BST[:, 128 * oc:128 * oc + 128],
                                       XAW[:, 512 * tn:512 * tn + 512],
                                       start=False, stop=True)
                        ins.then_inc(sem["mm"], 1)

            def fvb(e):
                for oc in range(16):
                    par = oc % 2
                    for tn in range(2):
                        j = 2 * oc + tn
                        e.wait_ge(sem["mm"], M0 + j + 1)
                        if j >= 1:
                            e.wait_ge(sem["vv"], V0 + j)
                        sl = slice(oc * NT + 512 * tn, oc * NT + 512 * tn + 512)
                        ins = e.scalar_tensor_tensor(
                            SC2[:, 0:512], pb[2 * par + tn][:, :],
                            SDN[:, 16 + oc:16 + oc + 1], XF[:, sl],
                            ALU.add, ALU.add)
                        t = b.sx(ins)
                        e.wait_ge(sem["sxv"], t)
                        ins = e.tensor_copy(XF[:, sl], SC2[:, 0:512])
                        ins.then_inc(sem["vv"], 1)

            b.phase({"gpsimd": fgb, "tensor": ftb, "vector": fvb})
            c["dg"] += 16 * 16
            c["mm"] += 32
            c["vv"] += 32

        emit_layernorm()
        dump("x4", XF[:, 0:16 * NT], F32, 16 * NT)

        S0 = c["ds"]

        def fst(e):
            e.dma_start(outx.ap().rearrange("(ch p) t -> p ch t", p=128),
                        XF[:].rearrange("p (ch t) -> p ch t", ch=16)
                        ).then_inc(sem["ds"], 16)
            e.wait_ge(sem["ds"], S0 + 16)

        b.phase({"sync": fst})
        c["ds"] += 16

    return nc


def _bf(x):
    return np.ascontiguousarray(x).astype(ml_dtypes.bfloat16)


def _host_inputs(inputs):
    f32 = np.float32
    sh = {}
    sh["wqdT"] = np.ascontiguousarray(inputs["L0_attn_Wq_down"].T).astype(f32)
    sh["wqupT"] = np.ascontiguousarray(inputs["L0_attn_Wq_up"].T).astype(f32)
    sh["wkvdT"] = np.ascontiguousarray(inputs["L0_attn_Wkv_down"].T).astype(f32)
    sh["kupw"] = np.ascontiguousarray(inputs["L0_attn_k_up"]).astype(f32)
    sh["vupflat"] = np.ascontiguousarray(
        inputs["L0_attn_v_up"].transpose(1, 0, 2).reshape(RKV, H * DH)).astype(f32)
    sh["woT0"] = np.ascontiguousarray(inputs["L0_attn_Wo"].T).astype(f32)
    sh["wg8"] = np.ascontiguousarray(np.asarray(inputs["L0_ffn_w_gate"], np.int8).reshape(16, 128, 64, 128).transpose(2, 1, 0, 3))
    sh["wu8"] = np.ascontiguousarray(np.asarray(inputs["L0_ffn_w_up"], np.int8).reshape(16, 128, 64, 128).transpose(2, 1, 0, 3))
    sh["wd8"] = np.ascontiguousarray(inputs["L0_ffn_w_down"]).astype(np.int8)
    sh["sgate"] = np.ascontiguousarray(
        inputs["L0_ffn_s_gate"].reshape(64, 128).T).astype(f32)
    sh["supv"] = np.ascontiguousarray(
        inputs["L0_ffn_s_up"].reshape(64, 128).T).astype(f32)
    sh["sdownp"] = np.ascontiguousarray(
        inputs["L0_ffn_s_down"].reshape(16, 128).T).astype(f32)
    sh["wq1"] = np.ascontiguousarray(inputs["L1_attn_Wq"]).astype(f32)
    sh["wk1"] = np.ascontiguousarray(inputs["L1_attn_Wk"]).astype(f32)
    sh["wv1"] = np.ascontiguousarray(inputs["L1_attn_Wv"]).astype(f32)
    sh["wo1"] = np.ascontiguousarray(inputs["L1_attn_Wo"]).astype(f32)
    sh["wT1"] = np.ascontiguousarray(inputs["L1_ffn_W"].T).astype(f32)
    sh["bP"] = np.ascontiguousarray(inputs["L1_ffn_b"].reshape(16, 128).T).astype(f32)
    sh["aTw"] = np.ascontiguousarray(
        inputs["L1_ffn_lora_A"].reshape(128, D).T).astype(f32)
    sh["bstackT"] = np.ascontiguousarray(
        inputs["L1_ffn_lora_B"].transpose(0, 2, 1).reshape(128, D)).astype(f32)
    sh["gwT"] = np.ascontiguousarray(inputs["L1_ffn_gate_w"].T).astype(f32)
    sh["gbrep"] = np.ascontiguousarray(
        np.tile(inputs["L1_ffn_gate_b"][None, :], (128, 1))).astype(f32)

    ki = np.arange(128)
    m4 = []
    for i in range(4):
        m4.append((np.arange(512)[None, :] < (ki[:, None] + 128 * i)))
    sh["masksw"] = np.ascontiguousarray(np.concatenate(m4, axis=1).astype(np.uint8))
    pm = np.zeros((128, 128), f32)
    for d in range(128):
        pm[(d - 1) % 128, d] = 1.0
    sh["pmatw"] = _bf(pm)
    es = np.zeros((8, 128), f32)
    for e in range(8):
        es[e, e * 16:(e + 1) * 16] = 1.0
    sh["eselw"] = _bf(es)
    sh["idnw"] = _bf(np.eye(128, dtype=f32))
    sh["ones1"] = _bf(np.ones((1, 128), f32))
    sh["onesm"] = _bf(np.ones((128, 128), f32))
    sh["onescol"] = _bf(np.ones((128, 1), f32))
    sh["invd"] = _bf(np.full((128, 1), 1.0 / D, f32))
    sh["epsap"] = np.full((1, 1), EPS, f32)

    x = np.asarray(inputs["L0_attn_hidden_states"], dtype=f32)
    percore = []
    for cc in range(NCORES):
        bidx, m = divmod(cc, 4)
        s0, s1 = stripes_of(m)
        pos = np.concatenate([np.arange(512 * s0, 512 * s0 + 512),
                              np.arange(512 * s1, 512 * s1 + 512)])
        d = {}
        d["xT"] = np.ascontiguousarray(x[bidx][pos].T).astype(f32)
        inv_freq = 1.0 / (10000.0 ** (np.arange(0, DH, 2, dtype=f32) / DH))
        freqs = pos[:, None].astype(f32) * inv_freq[None, :]
        emb = np.concatenate([freqs, freqs], axis=-1)
        cosT = np.cos(emb).T.astype(f32)
        sinT = np.sin(emb).T.astype(f32)
        d["ropek"] = _bf(np.concatenate([cosT, sinT], axis=1))
        ramp = np.zeros((128, 2048), np.float64)
        for h in range(6):
            rp = 32 * (h % 3)
            rb = 1024 * (h // 3)
            rv = (SLOPES[h] / SCALE) * (511.0 - np.arange(512, dtype=np.float64))
            hi = rv.astype(ml_dtypes.bfloat16)
            lo = (rv - hi.astype(np.float64)).astype(ml_dtypes.bfloat16)
            ramp[rp, rb:rb + 512] = hi.astype(np.float64)
            ramp[rp, rb + 512:rb + 1024] = lo.astype(np.float64)
        d["rampt"] = _bf(ramp)
        d["ropeq"] = _bf(np.concatenate([cosT * SCALE, sinT * SCALE], axis=1))
        alb = np.full((128, H * 48), KILL, f32)
        for h in range(H):
            sl = SLOPES[h]
            for sig in range(2):
                sstr = (s0, s1)[sig]
                q_off = 512 * sstr + 511
                base = h * 48 + (0 if sig == 0 else 16)
                nsp = NS[0][sig]
                ng = nsp - 4
                for i in range(nsp):
                    if i < ng:
                        jt = i
                        if jt >= 4 * sstr:
                            continue
                        kpos = 128 * jt + ki
                    else:
                        w = i - ng
                        jt = 4 * sstr + w
                        kpos = 128 * jt + ki
                    alb[:, base + i] = sl * (kpos - q_off)
        d["alibi"] = np.ascontiguousarray(alb)
        percore.append(d)
    return sh, percore


def _assemble(results):
    out = np.empty((B, L, D), np.float32)
    for cc in range(NCORES):
        bidx, m = divmod(cc, 4)
        s0, s1 = stripes_of(m)
        o = results[cc]["outx"]
        out[bidx, 512 * s0:512 * s0 + 512, :] = o[:, 0:512].T
        out[bidx, 512 * s1:512 * s1 + 512, :] = o[:, 512:1024].T
    return out


def kernel(**inputs):
    sh, percore = _host_inputs(inputs)
    nc = build_program()
    in_maps = []
    for cc in range(NCORES):
        d = dict(sh)
        d.update(percore[cc])
        in_maps.append(d)
    res = run_bass_kernel_spmd(nc, in_maps, core_ids=list(range(NCORES)))
    return _assemble(res.results)


# version 09
# speedup vs baseline: 1.0379x; 1.0379x over previous
import math
import contextlib
import numpy as np
import ml_dtypes
import concourse.bass as bass
import concourse.mybir as mybir
from concourse.bass_utils import run_bass_kernel_spmd

F32 = mybir.dt.float32
BF16 = mybir.dt.bfloat16
I8 = mybir.dt.int8
AF = mybir.ActivationFunctionType
ALU = mybir.AluOpType
AX = mybir.AxisListType

NCORES = 8
B, L, D = 2, 4096, 2048
H, DH = 16, 128
RQ, RKV = 1536, 512
INTER = 8192
NT = 1024
SCALE = 1.0 / math.sqrt(DH)
_ABASE = 2.0 ** (-2.0 ** (-(math.log2(H) - 3)))
SLOPES = [_ABASE ** (i + 1) for i in range(H)]
EPS = 1e-5
KILL = -1.0e30

ENGS = ("tensor", "vector", "scalar", "gpsimd", "sync")
GROUPS = [[0, 1, 2, 3], [4, 5, 6, 7]]

NSG = {0: 28, 1: 32}
NS = {0: (16, 32), 1: (32, 32)}


def stripes_of(m):
    return (m, 7 - m)


class Bld:
    def __init__(self, nc, stack):
        self.nc = nc
        self.sem = {}
        self.c = {}
        for name in ("bar", "dg", "ds", "dq", "da", "mm", "vv", "aa", "cc", "sxv", "sxa", "mr"):
            self.sem[name] = stack.enter_context(nc.semaphore(name))
            self.c[name] = 0

    def sx(self, ins, which="sxv"):
        self.c[which] += 1
        ins.then_inc(self.sem[which], 1)
        return self.c[which]

    def phase(self, fns):
        target = self.c["bar"]
        with self.nc.Block() as blk:
            for eng in ENGS:
                fn = fns.get(eng)

                def wrap(e, fn=fn):
                    e.wait_ge(self.sem["bar"], target)
                    if fn is not None:
                        fn(e)
                    e.drain()
                    e.sem_inc(self.sem["bar"], 1)

                getattr(blk, eng)(wrap)
        self.c["bar"] += 5


def build_program(dbg=()):
    nc = bass.Bass("TRN2", target_bir_lowering=False)
    dt = nc.dram_tensor

    xT = dt("xT", [D, NT], F32, kind="ExternalInput")
    wqdT = dt("wqdT", [D, RQ], F32, kind="ExternalInput")
    wqupT = dt("wqupT", [RQ, D], F32, kind="ExternalInput")
    wkvdT = dt("wkvdT", [D, RKV], F32, kind="ExternalInput")
    kupw = dt("kupw", [H, RKV, DH], F32, kind="ExternalInput")
    vupflat = dt("vupflat", [RKV, H * DH], F32, kind="ExternalInput")
    woT0 = dt("woT0", [D, D], F32, kind="ExternalInput")
    wg8 = dt("wg8", [64, 128, 16, 128], I8, kind="ExternalInput")
    wu8 = dt("wu8", [64, 128, 16, 128], I8, kind="ExternalInput")
    wd8 = dt("wd8", [INTER, D], I8, kind="ExternalInput")
    sgate = dt("sgate", [128, 64], F32, kind="ExternalInput")
    supv = dt("supv", [128, 64], F32, kind="ExternalInput")
    sdownp = dt("sdownp", [128, 16], F32, kind="ExternalInput")
    wq1 = dt("wq1", [D, D], F32, kind="ExternalInput")
    wk1 = dt("wk1", [D, D], F32, kind="ExternalInput")
    wv1 = dt("wv1", [D, D], F32, kind="ExternalInput")
    wo1 = dt("wo1", [D, D], F32, kind="ExternalInput")
    wT1 = dt("wT1", [D, D], F32, kind="ExternalInput")
    bP = dt("bP", [128, 16], F32, kind="ExternalInput")
    aTw = dt("aTw", [D, 128], F32, kind="ExternalInput")
    bstackT = dt("bstackT", [128, D], F32, kind="ExternalInput")
    gwT = dt("gwT", [D, 8], F32, kind="ExternalInput")
    gbrep = dt("gbrep", [128, 8], F32, kind="ExternalInput")
    rampt = dt("rampt", [128, 2048], BF16, kind="ExternalInput")
    ropeq = dt("ropeq", [128, 2 * NT], BF16, kind="ExternalInput")
    ropek = dt("ropek", [128, 2 * NT], BF16, kind="ExternalInput")
    alibi = dt("alibi", [128, H * 48], F32, kind="ExternalInput")
    masksw = dt("masksw", [128, 4 * 512], mybir.dt.uint8, kind="ExternalInput")
    pmatw = dt("pmatw", [128, 128], BF16, kind="ExternalInput")
    eselw = dt("eselw", [8, 128], BF16, kind="ExternalInput")
    idnw = dt("idnw", [128, 128], BF16, kind="ExternalInput")
    ones1 = dt("ones1", [1, 128], BF16, kind="ExternalInput")
    onesm = dt("onesm", [128, 128], BF16, kind="ExternalInput")
    onescol = dt("onescol", [128, 1], BF16, kind="ExternalInput")
    invd = dt("invd", [128, 1], BF16, kind="ExternalInput")
    epsap = dt("epsap", [1, 1], F32, kind="ExternalInput")

    outx = dt("outx", [D, NT], F32, kind="ExternalOutput")
    dbg_list = list(dbg)
    dbg_out = {}

    ag_kin = [dt(f"ag_kin{l}", [D, NT], BF16) for l in range(2)]
    ag_kout = [dt(f"ag_kout{l}", [4 * D, NT], BF16) for l in range(2)]
    ag_vin = [dt(f"ag_vin{l}", [NT, D], BF16) for l in range(2)]
    ag_vout = [dt(f"ag_vout{l}", [4 * NT, D], BF16) for l in range(2)]

    with contextlib.ExitStack() as S:
        b = Bld(nc, S)
        sem, c = b.sem, b.c
        _names = {}

        def sb(name, shape, dtype):
            n = _names.get(name, 0)
            _names[name] = n + 1
            return nc.sbuf_tensor(f"{name}_{n}" if n else name, shape, dtype)

        XF = S.enter_context(sb("XF", [128, 16 * NT], F32))
        XB = S.enter_context(sb("XB", [128, 16 * NT], BF16))
        SC2 = S.enter_context(sb("SC2", [128, 512], F32))
        SC3 = S.enter_context(sb("SC3", [128, 512], F32))
        CST = S.enter_context(sb("CST", [128, 800], F32))
        CSB = S.enter_context(sb("CSB", [128, 512], BF16))
        MSK8 = S.enter_context(sb("MSK8", [128, 2048], mybir.dt.uint8))
        NEGC = S.enter_context(sb("NEGC", [128, 512], BF16))
        ONESM = S.enter_context(sb("ONESM", [128, 128], BF16))
        E8 = S.enter_context(sb("E8", [8, 128 + NT], BF16))
        RCPB = S.enter_context(sb("RCPB", [1, 512], BF16))
        SDN = S.enter_context(sb("SDN", [128, 32], F32))

        pb = [S.enter_context(nc.psum_tensor(f"pb{i}", [128, 512], F32))
              for i in range(4)]

        EPSV = CST[0:1, 768:769]
        MASK = [MSK8[:, 512 * i:512 * (i + 1)] for i in range(4)]
        PMAT = CSB[:, 0:128]
        ONESC = CSB[:, 128:129]
        INVD = CSB[:, 129:130]
        IDN = CSB[:, 130:258]
        ONESR = CSB[0:1, 258:386]
        ALB = CST[:, 0:768]

        def p0_sync(e):
            e.dma_start(XF[:].rearrange("p (ch t) -> p ch t", ch=16),
                        xT.ap().rearrange("(ch p) t -> p ch t", p=128)).then_inc(sem["ds"], 16)
            e.dma_start(CST[:, 0:768], alibi.ap()).then_inc(sem["ds"], 16)
            e.dma_start(CST[0:1, 768:769], epsap.ap()).then_inc(sem["ds"], 16)
            e.dma_start(MSK8[:, 0:2048], masksw.ap()).then_inc(sem["ds"], 16)
            e.dma_start(CSB[:, 0:128], pmatw.ap()).then_inc(sem["ds"], 16)
            e.dma_start(CSB[:, 128:129], onescol.ap()).then_inc(sem["ds"], 16)
            e.dma_start(CSB[:, 129:130], invd.ap()).then_inc(sem["ds"], 16)
            e.dma_start(CSB[:, 130:258], idnw.ap()).then_inc(sem["ds"], 16)
            e.dma_start(CSB[0:1, 258:386], ones1.ap()).then_inc(sem["ds"], 16)
            e.dma_start(E8[:, 0:128], eselw.ap()).then_inc(sem["ds"], 16)
            e.dma_start(ONESM[:], onesm.ap()).then_inc(sem["ds"], 16)
            e.dma_start(SDN[:, 0:16], sdownp.ap()).then_inc(sem["ds"], 16)
            e.dma_start(SDN[:, 16:32], bP.ap()).then_inc(sem["ds"], 16)
            e.wait_ge(sem["ds"], c["ds"] + 13 * 16)

        def p0_g(e):
            e.dma_start(XB[:].rearrange("p (ch t) -> p ch t", ch=16),
                        xT.ap().rearrange("(ch p) t -> p ch t", p=128)).then_inc(sem["dg"], 16)
            e.wait_ge(sem["dg"], c["dg"] + 16)

        def p0_v(e):
            e.memset(NEGC[:, :], -1.0e30)

        b.phase({"sync": p0_sync, "gpsimd": p0_g, "vector": p0_v})
        c["ds"] += 13 * 16
        c["dg"] += 16

        def xb_rhs(dc, tn):
            return XB[:, dc * NT + 512 * tn: dc * NT + 512 * tn + 512]

        def emit_proj(wd_ap, Kc, Mt, rhs_fn, evac_fn, name, wbuf=None):
            with contextlib.ExitStack() as PS:
                if wbuf is None:
                    W = PS.enter_context(sb("W_" + name, [128, Kc * Mt * 128], BF16))
                else:
                    W = wbuf
                G0, Q0, M0, V0 = c["dg"], c["dq"], c["mm"], c["vv"]
                qc = Kc // 4
                wr = wd_ap.rearrange("(kc p) o -> p kc o", p=128)

                def fg(e):
                    e.dma_start(
                        W[:, 0:qc * Mt * 128].rearrange("p (kc o) -> p kc o", kc=qc),
                        wr[:, 0:qc, :]).then_inc(sem["dq"], 16)
                    e.dma_start(
                        W[:, qc * Mt * 128:Kc * Mt * 128].rearrange(
                            "p (kc o) -> p kc o", kc=Kc - qc),
                        wr[:, qc:Kc, :]).then_inc(sem["dg"], 16)

                def ft(e):
                    e.wait_ge(sem["dq"], Q0 + 16)
                    for mc in range(Mt):
                        for tn in range(2):
                            i = 2 * mc + tn
                            if i >= 4:
                                e.wait_ge(sem["vv"], V0 + i - 3)
                            for dc in range(Kc):
                                if i == 0 and dc == qc:
                                    e.wait_ge(sem["dg"], G0 + 16)
                                ins = e.matmul(
                                    pb[i % 4][:, :],
                                    W[:, (dc * Mt + mc) * 128:(dc * Mt + mc) * 128 + 128],
                                    rhs_fn(dc, tn),
                                    start=(dc == 0), stop=(dc == Kc - 1))
                            ins.then_inc(sem["mm"], 1)

                def fv(e):
                    for mc in range(Mt):
                        for tn in range(2):
                            i = 2 * mc + tn
                            e.wait_ge(sem["mm"], M0 + i + 1)
                            evac_fn(e, mc, tn, pb[i % 4][:, :])

                b.phase({"gpsimd": fg, "tensor": ft, "vector": fv})
                c["dg"] += 16
                c["dq"] += 16
                c["mm"] += 2 * Mt
                c["vv"] += 2 * Mt

        def copy_evac(dest_fn):
            def f(e, mc, tn, ps):
                e.tensor_copy(dest_fn(mc, tn), ps).then_inc(sem["vv"], 1)
            return f

        def emit_proj_split(wd_ap, Kc, Mt, nsplit, rhs_fn, evac_fn, name):
            Mh = Mt // nsplit
            for sp in range(nsplit):
                def ev(e, mc, tn, ps, sp=sp):
                    evac_fn(e, mc + Mh * sp, tn, ps)
                emit_proj(wd_ap[:, Mh * 128 * sp:Mh * 128 * (sp + 1)], Kc, Mh,
                          rhs_fn, ev, f"{name}_{sp}")

        def res_evac(e, mc, tn, ps):
            sl = slice(mc * NT + 512 * tn, mc * NT + 512 * tn + 512)
            e.tensor_add(XF[:, sl], XF[:, sl], ps).then_inc(sem["vv"], 1)

        def emit_layernorm():
            with contextlib.ExitStack() as PS:
                SC1 = PS.enter_context(sb("SC1", [128, 2048], BF16))
                SC1b = PS.enter_context(sb("SC1b", [128, 2048], BF16))
                M0, V0, A0 = c["mm"], c["vv"], c["aa"]

                def fv(e):
                    for cc in range(16):
                        if cc >= 2:
                            e.wait_ge(sem["mm"], M0 + cc - 1)
                        h = 1024 * (cc % 2)
                        e.tensor_copy(SC1[:, h:h + 1024],
                                      XF[:, cc * NT:(cc + 1) * NT]).then_inc(sem["vv"], 1)

                def fa(e):
                    for cc in range(16):
                        if cc >= 2:
                            e.wait_ge(sem["mm"], M0 + cc - 1)
                        h = 1024 * (cc % 2)
                        e.activation(SC1b[:, h:h + 1024],
                                     XF[:, cc * NT:(cc + 1) * NT],
                                     AF.Square).then_inc(sem["aa"], 1)

                def ft(e):
                    for cc in range(16):
                        e.wait_ge(sem["vv"], V0 + cc + 1)
                        e.wait_ge(sem["aa"], A0 + cc + 1)
                        h = 1024 * (cc % 2)
                        for tn in range(2):
                            e.matmul(pb[tn][0:1, :], INVD,
                                     SC1[:, h + 512 * tn:h + 512 * tn + 512],
                                     start=(cc == 0), stop=(cc == 15))
                        for tn in range(2):
                            ins = e.matmul(pb[2 + tn][0:1, :], INVD,
                                           SC1b[:, h + 512 * tn:h + 512 * tn + 512],
                                           start=(cc == 0), stop=(cc == 15))
                        ins.then_inc(sem["mm"], 1)

                b.phase({"vector": fv, "scalar": fa, "tensor": ft})
                c["mm"] += 16
                c["vv"] += 16
                c["aa"] += 16

                M1, V1, A1 = c["mm"], c["vv"], c["aa"]

                def fv2(e):
                    e.wait_ge(sem["mm"], M1)
                    ops = [
                        lambda: e.tensor_copy(SC2[0:1, 0:512], pb[0][0:1, :]),
                        lambda: e.tensor_copy(SC2[32:33, 0:512], pb[1][0:1, :]),
                        lambda: e.tensor_mul(SC3[32:33, 0:512], SC2[0:1, 0:512],
                                             SC2[0:1, 0:512]),
                        lambda: e.tensor_sub(SC2[64:65, 0:512], pb[2][0:1, :],
                                             SC3[32:33, 0:512]),
                        lambda: e.tensor_mul(SC3[32:33, 0:512], SC2[32:33, 0:512],
                                             SC2[32:33, 0:512]),
                        lambda: e.tensor_sub(SC3[0:1, 0:512], pb[3][0:1, :],
                                             SC3[32:33, 0:512]),
                        lambda: e.tensor_copy(SC1[0:1, 0:512], SC2[0:1, 0:512]),
                        lambda: e.tensor_copy(SC1[0:1, 512:1024], SC2[32:33, 0:512]),
                    ]
                    ins = None
                    for op in ops:
                        if ins is not None:
                            e.wait_ge(sem["sxv"], b.sx(ins))
                        ins = op()
                    ins.then_inc(sem["vv"], 1)

                def fa2(e):
                    e.wait_ge(sem["vv"], V1 + 1)
                    ops = [
                        lambda: e.activation(SC3[32:33, 0:512], SC2[64:65, 0:512],
                                             AF.Ln, bias=EPSV),
                        lambda: e.activation(SC1[0:1, 1024:1536], SC3[32:33, 0:512],
                                             AF.Exp, scale=-0.5),
                        lambda: e.activation(SC3[32:33, 0:512], SC3[0:1, 0:512],
                                             AF.Ln, bias=EPSV),
                        lambda: e.activation(SC1[0:1, 1536:2048], SC3[32:33, 0:512],
                                             AF.Exp, scale=-0.5),
                    ]
                    ins = None
                    for op in ops:
                        if ins is not None:
                            e.wait_ge(sem["sxa"], b.sx(ins, "sxa"))
                        ins = op()
                    ins.then_inc(sem["aa"], 1)

                b.phase({"vector": fv2, "scalar": fa2})
                c["vv"] += 1
                c["aa"] += 1

                M2, V2, A2 = c["mm"], c["vv"], c["aa"]

                def ft3(e):
                    e.wait_ge(sem["aa"], A2)
                    for tn in range(2):
                        e.matmul(pb[tn][:, :], ONESR, SC1[0:1, 512 * tn:512 * tn + 512],
                                 start=True, stop=True)
                        ins = e.matmul(pb[2 + tn][:, :], ONESR,
                                       SC1[0:1, 1024 + 512 * tn:1024 + 512 * tn + 512],
                                       start=True, stop=True)
                    ins.then_inc(sem["mm"], 1)

                def fv3(e):
                    e.wait_ge(sem["mm"], M2 + 1)
                    for cc in range(16):
                        for tn in range(2):
                            sl = slice(cc * NT + 512 * tn, cc * NT + 512 * tn + 512)
                            t = b.sx(e.tensor_sub(XF[:, sl], XF[:, sl], pb[tn][:, :]))
                            e.wait_ge(sem["sxv"], t)
                            t = b.sx(e.tensor_mul(XF[:, sl], XF[:, sl],
                                                  pb[2 + tn][:, :]))
                            e.wait_ge(sem["sxv"], t)
                    for cc in range(16):
                        ins = e.tensor_copy(XB[:, cc * NT:(cc + 1) * NT],
                                            XF[:, cc * NT:(cc + 1) * NT])
                    ins.then_inc(sem["vv"], 1)

                b.phase({"tensor": ft3, "vector": fv3})
                c["mm"] += 1
                c["vv"] += 1

        def emit_ag(layer, which, src_ap_fn):
            S0 = c["ds"]

            def fs(e):
                if which == "k":
                    e.dma_start(ag_kin[layer].ap().rearrange("(hh p) t -> p hh t", p=128),
                                src_ap_fn().rearrange("p (hh t) -> p hh t", hh=16)
                                ).then_inc(sem["ds"], 16)
                else:
                    e.dma_start(ag_vin[layer].ap().rearrange("(tt p) d2 -> p tt d2", p=128),
                                src_ap_fn().rearrange("p (tt d2) -> p tt d2", tt=8)
                                ).then_inc(sem["ds"], 16)

            def fg(e):
                e.wait_ge(sem["ds"], S0 + 16)
                src = ag_kin if which == "k" else ag_vin
                dst = ag_kout if which == "k" else ag_vout
                e.collective_compute("AllGather", ALU.bypass, GROUPS,
                                     [src[layer].ap().opt()],
                                     [dst[layer].ap().opt()]).then_inc(sem["cc"])

            b.phase({"sync": fs, "gpsimd": fg})
            c["ds"] += 16
            c["cc"] += 1

        def emit_rope(DST, TBL):
            M0, V0 = c["mm"], c["vv"]

            def ft(e):
                for i in range(32):
                    hc, tn = divmod(i, 2)
                    if i >= 2:
                        e.wait_ge(sem["vv"], V0 + i - 1)
                    ins = e.matmul(pb[i % 2][:, :], PMAT,
                                   DST[:, hc * NT + 512 * tn: hc * NT + 512 * tn + 512],
                                   start=True, stop=True)
                    ins.then_inc(sem["mm"], 1)

            def fv(e):
                for i in range(32):
                    hc, tn = divmod(i, 2)
                    sl = slice(hc * NT + 512 * tn, hc * NT + 512 * tn + 512)
                    e.wait_ge(sem["mm"], M0 + i + 1)
                    if i >= 1:
                        e.wait_ge(sem["vv"], V0 + i)
                    t = b.sx(e.tensor_mul(SC2[:, 0:512], DST[:, sl],
                                          TBL[:, 512 * tn:512 * tn + 512]))
                    e.wait_ge(sem["sxv"], t)
                    t = b.sx(e.tensor_mul(SC3[:, 0:512], pb[i % 2][:, :],
                                          TBL[:, 1024 + 512 * tn:1024 + 512 * tn + 512]))
                    e.wait_ge(sem["sxv"], t)
                    ins = e.tensor_add(DST[:, sl], SC2[:, 0:512], SC3[:, 0:512])
                    ins.then_inc(sem["vv"], 1)

            b.phase({"tensor": ft, "vector": fv})
            c["mm"] += 32
            c["vv"] += 32

        RAMPH = 6

        def emit_attention(layer, QTA, PTA, KVA, SPA, SPB, RAMP=None):
            kg = ag_kout[layer].ap()
            vg = ag_vout[layer].ap()
            nsg = NSG[layer]
            ntil = nsg + (8 if layer == 0 else 0)
            span = 2 * ntil * 128
            ns0, ns1 = NS[layer]

            def KTo(ph):
                return ph * span

            def VTo(ph):
                return ph * span + ntil * 128

            def slot_cols(sig, i):
                nsp = (ns0 if sig == 0 else ns1)
                ng = nsp - (4 if layer == 0 else 0)
                if layer == 0 and i >= ng:
                    return (nsg + 4 * sig + (i - ng)) * 128
                return i * 128

            S0, S0a, M0, V0, A0, C0 = (c["ds"], c["da"], c["mm"], c["vv"],
                                        c["aa"], c["cc"])
            epp = 1 if layer == 0 else 2
            mm_h = (ns0 + 1) + (ns1 + 1)
            aa_h = (ns0 // epp + 1) + (ns1 // epp + 1)
            nmask = 4 if layer == 0 else 0
            PBK = nmask + 3
            vv_h = 2 * PBK

            def mm_base(h, sig):
                return M0 + h * mm_h + (ns0 + 1 if sig else 0)

            def aa_base(h, sig):
                return A0 + h * aa_h + (ns0 // epp + 1 if sig else 0)

            def vv_base(h, sig):
                return V0 + h * vv_h + (PBK if sig else 0)

            nstripe = 7 if layer == 0 else 8
            ndma_h = 2 * nstripe + (2 if layer == 0 else 0)
            G0r = c["dg"]

            def fgr(e):
                e.dma_start(RAMP[:], rampt.ap()).then_inc(sem["dg"], 16)

            def fs(e):
                e.wait_ge(sem["cc"], C0)
                for h in range(H):
                    ph = h % 2
                    psem = sem["ds"] if ph == 0 else sem["da"]
                    if h >= 2:
                        e.wait_ge(sem["mm"], mm_base(h - 1, 0))
                    kga = kg.rearrange("(r hh p) (cb t) -> r hh p cb t",
                                       r=4, hh=H, p=128, cb=8, t=128)
                    vga = vg.rearrange("(r cb p) d2 -> r cb p d2",
                                       r=4, cb=8, p=128)
                    for s in range(nstripe):
                        r = s if s < 4 else 7 - s
                        cb0 = 0 if s < 4 else 4
                        e.dma_start(
                            KVA[:, KTo(ph) + 512 * s:KTo(ph) + 512 * s + 512].rearrange(
                                "p (cb t) -> p cb t", cb=4, t=128),
                            kga[r, h][:, cb0:cb0 + 4, :]).then_inc(psem, 16)
                        e.dma_start(
                            KVA[:, VTo(ph) + 512 * s:VTo(ph) + 512 * s + 512].rearrange(
                                "p (cb d2) -> p cb d2", cb=4, d2=128),
                            vga[r, cb0:cb0 + 4, :, 128 * h:128 * h + 128].rearrange(
                                "cb p d2 -> p cb d2")).then_inc(psem, 16)
                    if layer == 0:
                        e.dma_start(
                            KVA[:, KTo(ph) + nsg * 128:KTo(ph) + nsg * 128 + 1024],
                            ag_kin[0].ap().rearrange("(hh p) t -> hh p t",
                                                     hh=H, p=128)[h]
                        ).then_inc(psem, 16)
                        e.dma_start(
                            KVA[:, VTo(ph) + nsg * 128:VTo(ph) + nsg * 128 + 1024].rearrange(
                                "p (cb d2) -> p cb d2", cb=8, d2=128),
                            ag_vin[0].ap().rearrange("(cb p) d2 -> p cb d2",
                                                     cb=8, p=128)
                            [:, :, 128 * h:128 * h + 128]).then_inc(psem, 16)

            R0 = c["mr"]

            def ft(e):
                pending = []

                def flush_repl():
                    while pending:
                        aaw, vvw, nrep = pending.pop(0)
                        e.wait_ge(sem["aa"], aaw)
                        e.wait_ge(sem["vv"], vvw)
                        ins = e.matmul(pb[3][:, :], ONESR, RCPB[0:1, :],
                                       start=True, stop=True)
                        ins.then_inc(sem["mr"], 1)

                if layer == 0:
                    e.wait_ge(sem["dg"], G0r + 16)
                for h in range(H):
                    ph = h % 2
                    if ph == 0:
                        e.wait_ge(sem["ds"], S0 + 16 * ndma_h * (h // 2 + 1))
                    else:
                        e.wait_ge(sem["da"], S0a + 16 * ndma_h * (h // 2 + 1))
                    for sig in range(2):
                        blk = 2 * h + sig
                        nsp = ns0 if sig == 0 else ns1
                        mmb = mm_base(h, sig)
                        aab = aa_base(h, sig)
                        vvb = vv_base(h, sig)
                        obank = pb[0] if blk % 2 == 0 else pb[1]
                        qr = QTA[:, h * NT + 512 * sig: h * NT + 512 * sig + 512]

                        def sap(i):
                            SP = SPA if (i // 2) % 2 == 0 else SPB
                            return SP[:, 512 * (i % 2):512 * (i % 2) + 512]
                        use_ramp = (layer == 0 and h < RAMPH)
                        if use_ramp:
                            rp = 32 * (h % 3)
                            rb = 1024 * (h // 3)
                            rhi = RAMP[rp:rp + 1, rb:rb + 512]
                            rlo = RAMP[rp:rp + 1, rb + 512:rb + 1024]

                        def s_chunk(i):
                            if i >= 4:
                                e.wait_ge(sem["aa"], aab + (i - 4) // epp + 1)
                            elif aab > A0:
                                e.wait_ge(sem["aa"], aab - 1)
                            sb_ = sap(i)
                            ins = e.matmul(sb_,
                                           KVA[:, KTo(ph) + slot_cols(sig, i):
                                               KTo(ph) + slot_cols(sig, i) + 128],
                                           qr, start=True, stop=not use_ramp)
                            if use_ramp:
                                e.matmul(sb_, ONESM[rp:rp + 1, :], rhi,
                                         start=False, stop=False)
                                ins = e.matmul(sb_, ONESM[rp:rp + 1, :],
                                               rlo, start=False, stop=True)
                            ins.then_inc(sem["mm"], 1)

                        def pvpair(i0):
                            e.wait_ge(sem["aa"], aab + (i0 + 1) // epp + 1)
                            if i0 == 0:
                                e.wait_ge(sem["vv"], vvb)
                            for j in (i0, i0 + 1):
                                pt = PTA[:, 512 * (j % 4):512 * (j % 4) + 512]
                                if j % 2 == 0:
                                    e.matmul(pb[2][0:1, :], ONESC, pt,
                                             start=(j == 0), stop=(j == nsp - 2),
                                             tile_position=(0, 0))
                                else:
                                    e.matmul(pb[3][64:65, :], ONESC, pt,
                                             start=(j == 1), stop=(j == nsp - 1),
                                             tile_position=(0, 64))
                            for j in (i0, i0 + 1):
                                pt = PTA[:, 512 * (j % 4):512 * (j % 4) + 512]
                                ins = e.matmul(obank[:, :],
                                               KVA[:, VTo(ph) + slot_cols(sig, j):
                                                   VTo(ph) + slot_cols(sig, j) + 128],
                                               pt, start=(j == 0),
                                               stop=(j == nsp - 1))
                            if i0 + 1 == nsp - 1:
                                ins.then_inc(sem["mm"], 1)

                        for i in range(nsp):
                            s_chunk(i)
                            if i == 3:
                                flush_repl()
                            if i % 2 == 1 and i >= 3:
                                pvpair(i - 3)
                        pvpair(nsp - 2)
                        pending.append((aab + nsp // epp + 1, vvb, 0))
                if pending:
                    flush_repl()

            def fa(e):
                for h in range(H):
                    for sig in range(2):
                        nsp = ns0 if sig == 0 else ns1
                        mmb = mm_base(h, sig)
                        vvb = vv_base(h, sig)

                        def sapf(i):
                            SP = SPA if (i // 2) % 2 == 0 else SPB
                            return SP[:, 512 * (i % 2):512 * (i % 2) + 512]

                        if layer == 0:
                            for i in range(nsp):
                                e.wait_ge(sem["mm"], mmb + i + 1)
                                if i >= nsp - 4:
                                    e.wait_ge(sem["vv"], vvb + (i - (nsp - 4)) + 1)
                                bias_ap = ALB[:, h * 48 + (0 if sig == 0 else 16) + i:
                                              h * 48 + (0 if sig == 0 else 16) + i + 1]
                                ins = e.activation(PTA[:, 512 * (i % 4):512 * (i % 4) + 512],
                                                   sapf(i), AF.Exp,
                                                   bias=bias_ap, scale=SCALE)
                                ins.then_inc(sem["aa"], 1)
                        else:
                            for j in range(nsp // 2):
                                e.wait_ge(sem["mm"], mmb + 2 * j + 2)
                                SP = SPA if j % 2 == 0 else SPB
                                ins = e.activation(PTA[:, 1024 * (j % 2):1024 * (j % 2) + 1024],
                                                   SP[:, :], AF.Exp)
                                ins.then_inc(sem["aa"], 1)
                        e.wait_ge(sem["vv"], vvb + nmask + 1)
                        t = b.sx(e.activation(SC3[64:65, 0:512], SC2[32:33, 0:512],
                                              AF.Ln), "sxa")
                        e.wait_ge(sem["sxa"], t)
                        ins = e.activation(RCPB[0:1, :], SC3[64:65, 0:512],
                                           AF.Exp, scale=-1.0)
                        ins.then_inc(sem["aa"], 1)

            def fv(e):
                for h in range(H):
                    for sig in range(2):
                        blk = 2 * h + sig
                        nsp = ns0 if sig == 0 else ns1
                        mmb = mm_base(h, sig)
                        aab = aa_base(h, sig)
                        obank = pb[0] if blk % 2 == 0 else pb[1]
                        if layer == 0:
                            for ii in range(4):
                                i = nsp - 4 + ii
                                e.wait_ge(sem["mm"], mmb + i + 1)
                                SP = SPA if (i // 2) % 2 == 0 else SPB
                                ins = e.copy_predicated(
                                    SP[:, 512 * (i % 2):512 * (i % 2) + 512],
                                    MASK[ii], NEGC[:, :])
                                ins.then_inc(sem["vv"], 1)
                        e.wait_ge(sem["mm"], mmb + nsp + 1)
                        t = b.sx(e.tensor_copy(SC2[0:1, 0:512], pb[3][64:65, :]))
                        e.wait_ge(sem["sxv"], t)
                        ins = e.tensor_add(SC2[32:33, 0:512], pb[2][0:1, :],
                                           SC2[0:1, 0:512])
                        ins.then_inc(sem["vv"], 1)
                        e.wait_ge(sem["mr"], R0 + blk + 1)
                        ins = e.tensor_copy(SC3[:, 0:512], pb[3][:, :])
                        t = b.sx(ins)
                        e.wait_ge(sem["sxv"], t)
                        ins = e.tensor_mul(
                            XB[:, h * NT + 512 * sig: h * NT + 512 * sig + 512],
                            obank[:, :], SC3[:, 0:512])
                        ins.then_inc(sem["vv"], 2)

            fns = {"sync": fs, "tensor": ft, "scalar": fa, "vector": fv}
            if layer == 0:
                fns["gpsimd"] = fgr
            b.phase(fns)
            if layer == 0:
                c["dg"] += 16
            c["ds"] += 16 * ndma_h * (H // 2)
            c["da"] += 16 * ndma_h * (H // 2)
            c["mm"] += mm_h * H
            c["aa"] += aa_h * H
            c["vv"] += vv_h * H
            c["mr"] += 2 * H

        def dump(name, src_ap, dtype, cols):
            if name not in dbg_list:
                return
            od = dt("dbg_" + name, [128, cols], dtype, kind="ExternalOutput")
            dbg_out[name] = od
            S0 = c["ds"]

            def fs(e):
                e.dma_start(od.ap(), src_ap).then_inc(sem["ds"], 16)
                e.wait_ge(sem["ds"], S0 + 16)

            b.phase({"sync": fs})
            c["ds"] += 16

        with contextlib.ExitStack() as S1:
            CT = S1.enter_context(sb("CT", [128, 4 * NT], BF16))
            ARENA = S1.enter_context(sb("ARENA", [128, 16 * NT], BF16))

            emit_proj(wkvdT.ap(), 16, 4, xb_rhs,
                      copy_evac(lambda mc, tn: CT[:, mc * NT + 512 * tn:
                                                  mc * NT + 512 * tn + 512]), "kvd")
            dump("ct", CT[:, 0:4 * NT], BF16, 4 * NT)

            with contextlib.ExitStack() as S1k:
                KUP = S1k.enter_context(sb("KUP", [128, H * 4 * 128], BF16))
                M0, V0, G0 = c["mm"], c["vv"], c["dg"]

                def ftk(e):
                    e.wait_ge(sem["dg"], G0 + 16)
                    for h in range(H):
                        for tn in range(2):
                            i = 2 * h + tn
                            if i >= 4:
                                e.wait_ge(sem["vv"], V0 + i - 3)
                            for rc in range(4):
                                ins = e.matmul(
                                    pb[i % 4][:, :],
                                    KUP[:, (h * 4 + rc) * 128:(h * 4 + rc) * 128 + 128],
                                    CT[:, rc * NT + 512 * tn:rc * NT + 512 * tn + 512],
                                    start=(rc == 0), stop=(rc == 3))
                            ins.then_inc(sem["mm"], 1)

                def fvk(e):
                    for i in range(32):
                        h, tn = divmod(i, 2)
                        e.wait_ge(sem["mm"], M0 + i + 1)
                        e.tensor_copy(ARENA[:, h * NT + 512 * tn:h * NT + 512 * tn + 512],
                                      pb[i % 4][:, :]).then_inc(sem["vv"], 1)

                def fgk(e):
                    e.dma_start(KUP[:].rearrange("p (hh r dd) -> p hh r dd", hh=H, r=4),
                                kupw.ap().rearrange("hh (r p) dd -> p hh r dd", p=128)
                                ).then_inc(sem["dg"], 16)

                b.phase({"gpsimd": fgk, "tensor": ftk, "vector": fvk})
                c["dg"] += 16
                c["mm"] += 32
                c["vv"] += 32
            dump("k0", ARENA[:, 0:16 * NT], BF16, 16 * NT)
            emit_ag(0, "k", lambda: ARENA[:])

            with contextlib.ExitStack() as S1v:
                VUP = S1v.enter_context(sb("VUP", [128, 4 * D], BF16))
                M0, V0, G0 = c["mm"], c["vv"], c["dg"]

                def ftv(e):
                    e.wait_ge(sem["dg"], G0 + 16)
                    for i in range(32):
                        tt, oc = divmod(i, 4)
                        if i >= 4:
                            e.wait_ge(sem["vv"], V0 + i - 3)
                        for rc in range(4):
                            ins = e.matmul(
                                pb[i % 4][:, :],
                                CT[:, rc * NT + 128 * tt:rc * NT + 128 * tt + 128],
                                VUP[:, rc * D + 512 * oc:rc * D + 512 * oc + 512],
                                start=(rc == 0), stop=(rc == 3))
                        ins.then_inc(sem["mm"], 1)

                def fvv(e):
                    for i in range(32):
                        tt, oc = divmod(i, 4)
                        e.wait_ge(sem["mm"], M0 + i + 1)
                        e.tensor_copy(ARENA[:, tt * D + 512 * oc:tt * D + 512 * oc + 512],
                                      pb[i % 4][:, :]).then_inc(sem["vv"], 1)

                def fgv0(e):
                    e.dma_start(VUP[:].rearrange("p (r d2) -> p r d2", r=4),
                                vupflat.ap().rearrange("(r p) d2 -> p r d2", p=128)
                                ).then_inc(sem["dg"], 16)

                b.phase({"gpsimd": fgv0, "tensor": ftv, "vector": fvv})
                c["dg"] += 16
                c["mm"] += 32
                c["vv"] += 32
            dump("v0", ARENA[:, 0:8 * D], BF16, 8 * D)
            emit_ag(0, "v", lambda: ARENA[:, 0:8 * D])

        QTS = contextlib.ExitStack()
        QTA = QTS.enter_context(sb("QTA", [128, 16 * NT], BF16))
        with contextlib.ExitStack() as S2:
            QD = S2.enter_context(sb("QD", [128, 12 * NT], BF16))
            emit_proj_split(wqdT.ap(), 16, 12, 2, xb_rhs,
                            copy_evac(lambda mc, tn: QD[:, mc * NT + 512 * tn:
                                                        mc * NT + 512 * tn + 512]), "qd")

            def qd_rhs(rc, tn):
                return QD[:, rc * NT + 512 * tn: rc * NT + 512 * tn + 512]

            for mh in range(2):
                emit_proj(wqupT.ap()[:, 1024 * mh:1024 * (mh + 1)], 12, 8, qd_rhs,
                          copy_evac(lambda mc, tn, mh=mh:
                                    QTA[:, (mc + 8 * mh) * NT + 512 * tn:
                                        (mc + 8 * mh) * NT + 512 * tn + 512]),
                          f"qup{mh}")
        dump("q0", QTA[:, 0:16 * NT], BF16, 16 * NT)

        with contextlib.ExitStack() as S3:
            PTA = S3.enter_context(sb("PTA1", [128, 4 * 512], BF16))
            KVA = S3.enter_context(sb("KVA1", [128, 2 * 2 * 36 * 128], BF16))
            RAMP = S3.enter_context(sb("RAMP", [128, 2048], BF16))
            SPA = S3.enter_context(nc.psum_tensor("spa0", [128, 1024], F32))
            SPB = S3.enter_context(nc.psum_tensor("spb0", [128, 1024], F32))
            emit_attention(0, QTA, PTA, KVA, SPA, SPB, RAMP)
        QTS.close()
        dump("ao0", XB[:, 0:16 * NT], BF16, 16 * NT)

        emit_proj_split(woT0.ap(), 16, 16, 2, xb_rhs, res_evac, "wo0")

        with contextlib.ExitStack() as S5:
            FUSE = S5.enter_context(sb("FUSE", [128, 64 * 512], BF16))
            WG = S5.enter_context(sb("WG", [128, 2 * 2048], BF16))
            WU = S5.enter_context(sb("WU", [128, 2 * 2048], BF16))
            SGF = S5.enter_context(sb("SGF", [128, 128], F32))
            pd = [S5.enter_context(nc.psum_tensor(f"pd{i}", [128, 512], F32))
                  for i in range(4)]

            def fld(e):
                e.dma_start(SGF[:, 0:64], sgate.ap()).then_inc(sem["dg"], 16)
                e.dma_start(SGF[:, 64:128], supv.ap()).then_inc(sem["dg"], 16)
                for oc in range(2):
                    par = oc % 2
                    ps_ = sem["dg"] if par == 0 else sem["dq"]
                    e.dma_start(
                        WG[:, par * 2048:(par + 1) * 2048].rearrange(
                            "p (kc o) -> p kc o", kc=16),
                        wg8.ap()[oc]).then_inc(ps_, 16)
                    e.dma_start(
                        WU[:, par * 2048:(par + 1) * 2048].rearrange(
                            "p (kc o) -> p kc o", kc=16),
                        wu8.ap()[oc]).then_inc(ps_, 16)

            b.phase({"gpsimd": fld})
            c["dg"] += 32 + 32
            c["dq"] += 32
            emit_layernorm()
            dump("x1", XF[:, 0:16 * NT], F32, 16 * NT)
            WD = S5.enter_context(sb("WD", [128, 2 * 4096], BF16))

            for th in range(2):
                if th == 0:
                    G0, Q0 = c["dg"] - 32, c["dq"] - 32
                else:
                    G0, Q0 = c["dg"], c["dq"]
                M0, V0, A0 = c["mm"], c["vv"], c["aa"]

                def fg(e, th=th, M0=M0):
                    for oc in range(2 if th == 0 else 0, 64):
                        if oc >= 2:
                            e.wait_ge(sem["mm"], M0 + 2 * (oc - 1))
                        par = oc % 2
                        ps_ = sem["dg"] if par == 0 else sem["dq"]
                        e.dma_start(
                            WG[:, par * 2048:(par + 1) * 2048].rearrange(
                                "p (kc o) -> p kc o", kc=16),
                            wg8.ap()[oc]).then_inc(ps_, 16)
                        e.dma_start(
                            WU[:, par * 2048:(par + 1) * 2048].rearrange(
                                "p (kc o) -> p kc o", kc=16),
                            wu8.ap()[oc]).then_inc(ps_, 16)

                def ft(e, th=th, G0=G0, Q0=Q0, V0=V0, A0=A0):
                    for oc in range(64):
                        par = oc % 2
                        if par == 0:
                            e.wait_ge(sem["dg"], G0 + 32 * (oc // 2 + 1))
                        else:
                            e.wait_ge(sem["dq"], Q0 + 32 * (oc // 2 + 1))
                        if oc >= 2:
                            e.wait_ge(sem["aa"], A0 + oc - 1)
                            e.wait_ge(sem["vv"], V0 + oc - 1)
                        for dc in range(16):
                            e.matmul(pb[2 * par][:, :],
                                     WG[:, par * 2048 + dc * 128:par * 2048 + dc * 128 + 128],
                                     XB[:, dc * NT + 512 * th:dc * NT + 512 * th + 512],
                                     start=(dc == 0), stop=(dc == 15))
                        for dc in range(16):
                            ins = e.matmul(pb[2 * par + 1][:, :],
                                           WU[:, par * 2048 + dc * 128:par * 2048 + dc * 128 + 128],
                                           XB[:, dc * NT + 512 * th:dc * NT + 512 * th + 512],
                                           start=(dc == 0), stop=(dc == 15))
                        ins.then_inc(sem["mm"], 2)

                def fa(e, th=th, M0=M0, V0=V0):
                    for oc in range(64):
                        par = oc % 2
                        e.wait_ge(sem["mm"], M0 + 2 * (oc + 1))
                        if oc >= 1:
                            e.wait_ge(sem["vv"], V0 + oc)
                        ins = e.activation(SC3[:, 0:512], pb[2 * par][:, :],
                                           AF.Silu, scale=SGF[:, oc:oc + 1])
                        ins.then_inc(sem["aa"], 1)

                def fv(e, th=th, A0=A0):
                    for oc in range(64):
                        par = oc % 2
                        e.wait_ge(sem["aa"], A0 + oc + 1)
                        ins = e.scalar_tensor_tensor(
                            FUSE[:, 512 * oc:512 * oc + 512],
                            pb[2 * par + 1][:, :], SGF[:, 64 + oc:64 + oc + 1],
                            SC3[:, 0:512], ALU.mult, ALU.mult)
                        ins.then_inc(sem["vv"], 1)

                b.phase({"gpsimd": fg, "tensor": ft, "scalar": fa, "vector": fv})
                c["dg"] += 32 * 32 - (32 if th == 0 else 0)
                c["dq"] += 32 * 32 - (32 if th == 0 else 0)
                c["mm"] += 128
                c["aa"] += 64
                c["vv"] += 64
                dump(f"fuse{th}", FUSE[:, 0:64 * 512], BF16, 64 * 512)

                for half in range(2):
                    G0, Q0d, M0, V0 = c["dg"], c["dq"], c["mm"], c["vv"]

                    def fgd(e, G0=G0, M0=M0):
                        for g in range(32):
                            if g >= 2:
                                e.wait_ge(sem["mm"], M0 + 2 * g - 2)
                            par = g % 2
                            ps_ = sem["dg"] if par == 0 else sem["dq"]
                            e.dma_start(WD[:, par * 4096:(par + 1) * 4096].rearrange(
                                            "p (u q) -> p u q", u=2, q=2048),
                                        wd8.ap()[256 * g:256 * (g + 1), :].rearrange(
                                            "(u p) q -> p u q", p=128)
                                        ).then_inc(ps_, 16)

                    def ftd(e, half=half, G0=G0, Q0d=Q0d):
                        banks = pb[0:4] + pd[0:4]
                        for oc in range(64):
                            g, u = divmod(oc, 2)
                            par = g % 2
                            if u == 0:
                                if par == 0:
                                    e.wait_ge(sem["dg"], G0 + 16 * (g // 2 + 1))
                                else:
                                    e.wait_ge(sem["dq"], Q0d + 16 * (g // 2 + 1))
                            for k8 in range(8):
                                mc = 8 * half + k8
                                ins = e.matmul(
                                    banks[k8][:, :],
                                    WD[:, par * 4096 + u * 2048 + mc * 128:
                                       par * 4096 + u * 2048 + mc * 128 + 128],
                                    FUSE[:, 512 * oc:512 * oc + 512],
                                    start=(oc == 0), stop=(oc == 63))
                            ins.then_inc(sem["mm"], 1)

                    def fvd(e, th=th, half=half, M0=M0):
                        e.wait_ge(sem["mm"], M0 + 64)
                        banks = pb[0:4] + pd[0:4]
                        for k8 in range(8):
                            mc = 8 * half + k8
                            sl = slice(mc * NT + 512 * th, mc * NT + 512 * th + 512)
                            ins = e.scalar_tensor_tensor(
                                XF[:, sl], banks[k8][:, :], SDN[:, mc:mc + 1],
                                XF[:, sl], ALU.mult, ALU.add)
                            ins.then_inc(sem["vv"], 1)

                    b.phase({"gpsimd": fgd, "tensor": ftd, "vector": fvd})
                    c["dg"] += 16 * 16
                    c["dq"] += 16 * 16
                    c["mm"] += 64
                    c["vv"] += 8

        emit_layernorm()
        dump("x2", XF[:, 0:16 * NT], F32, 16 * NT)

        with contextlib.ExitStack() as S6:
            PTA2 = S6.enter_context(sb("PTA2", [128, 16 * NT], BF16))
            RKT = S6.enter_context(sb("RKT", [128, 2 * NT], BF16))
            S0k = c["ds"]

            def fldk(e):
                e.dma_start(RKT[:], ropek.ap()).then_inc(sem["ds"], 16)
                e.wait_ge(sem["ds"], S0k + 16)

            b.phase({"sync": fldk})
            c["ds"] += 16
            emit_proj_split(wk1.ap(), 16, 16, 2, xb_rhs,
                            copy_evac(lambda mc, tn: PTA2[:, mc * NT + 512 * tn:
                                                          mc * NT + 512 * tn + 512]), "wk1")
            emit_rope(PTA2, RKT)
            dump("k1", PTA2[:, 0:16 * NT], BF16, 16 * NT)
            emit_ag(1, "k", lambda: PTA2[:])

        with contextlib.ExitStack() as S7:
            KVA2 = S7.enter_context(sb("KVA2", [128, 8 * D], BF16))
            for vh in range(2):
                with contextlib.ExitStack() as S7b:
                    WV = S7b.enter_context(sb("WV", [128, 16 * NT], BF16))
                    G0, M0, V0 = c["dg"], c["mm"], c["vv"]

                    def fgv(e, vh=vh):
                        e.dma_start(WV[:].rearrange("p (kc o) -> p kc o", kc=16),
                                    wv1.ap()[:, NT * vh:NT * (vh + 1)].rearrange(
                                        "(kc p) o -> p kc o", p=128)
                                    ).then_inc(sem["dg"], 16)

                    def ftv1(e, vh=vh, G0=G0, V0=V0):
                        e.wait_ge(sem["dg"], G0 + 16)
                        for i in range(16):
                            tt, oc = divmod(i, 2)
                            if i >= 4:
                                e.wait_ge(sem["vv"], V0 + i - 3)
                            for dc in range(16):
                                ins = e.matmul(
                                    pb[i % 4][:, :],
                                    XB[:, dc * NT + 128 * tt:dc * NT + 128 * tt + 128],
                                    WV[:, dc * NT + 512 * oc:dc * NT + 512 * oc + 512],
                                    start=(dc == 0), stop=(dc == 15))
                            ins.then_inc(sem["mm"], 1)

                    def fvv1(e, vh=vh, M0=M0):
                        for i in range(16):
                            tt, oc = divmod(i, 2)
                            e.wait_ge(sem["mm"], M0 + i + 1)
                            e.tensor_copy(
                                KVA2[:, tt * D + NT * vh + 512 * oc:
                                     tt * D + NT * vh + 512 * oc + 512],
                                pb[i % 4][:, :]).then_inc(sem["vv"], 1)

                    b.phase({"gpsimd": fgv, "tensor": ftv1, "vector": fvv1})
                    c["dg"] += 16
                    c["mm"] += 16
                    c["vv"] += 16
            dump("v1", KVA2[:, 0:8 * D], BF16, 8 * D)
            emit_ag(1, "v", lambda: KVA2[:])

        QTS = contextlib.ExitStack()
        QTA = QTS.enter_context(sb("QTA2", [128, 16 * NT], BF16))
        with contextlib.ExitStack() as S8:
            RQT = S8.enter_context(sb("RQT", [128, 2 * NT], BF16))
            S0q = c["ds"]

            def fldq(e):
                e.dma_start(RQT[:], ropeq.ap()).then_inc(sem["ds"], 16)
                e.wait_ge(sem["ds"], S0q + 16)

            b.phase({"sync": fldq})
            c["ds"] += 16
            emit_proj_split(wq1.ap(), 16, 16, 2, xb_rhs,
                            copy_evac(lambda mc, tn: QTA[:, mc * NT + 512 * tn:
                                                         mc * NT + 512 * tn + 512]), "wq1")
            emit_rope(QTA, RQT)
        dump("q1", QTA[:, 0:16 * NT], BF16, 16 * NT)

        with contextlib.ExitStack() as S9:
            PTA = S9.enter_context(sb("PTA3", [128, 4 * 512], BF16))
            KVA = S9.enter_context(sb("KVA3", [128, 2 * 2 * 32 * 128], BF16))
            SPA = S9.enter_context(nc.psum_tensor("spa1", [128, 1024], F32))
            SPB = S9.enter_context(nc.psum_tensor("spb1", [128, 1024], F32))
            emit_attention(1, QTA, PTA, KVA, SPA, SPB)
        QTS.close()
        dump("ao1", XB[:, 0:16 * NT], BF16, 16 * NT)

        emit_proj_split(wo1.ap(), 16, 16, 2, xb_rhs, res_evac, "wo1")
        emit_layernorm()
        dump("x3", XF[:, 0:16 * NT], F32, 16 * NT)

        with contextlib.ExitStack() as SA:
            ATS = SA.enter_context(sb("ATS", [128, 16 * 128], BF16))
            BST = SA.enter_context(sb("BST", [128, 16 * 128], BF16))
            GWT = SA.enter_context(sb("GWT", [128, 16 * 8], BF16))
            GBR = SA.enter_context(sb("GBR", [128, 8], F32))
            WT = SA.enter_context(sb("WTb", [128, 2 * 16 * 128], BF16))
            G1 = SA.enter_context(sb("G1", [128, 64], BF16))
            GS = SA.enter_context(sb("GS", [128, 8], F32))
            XAW = SA.enter_context(sb("XAW", [128, NT], BF16))
            WRB = SA.enter_context(sb("WRB", [128, NT], BF16))
            pbh = SA.enter_context(nc.psum_tensor("pbh", [128, 1024], BF16))
            PXL = SA.enter_context(nc.psum_tensor("pxl", [128, 512], F32))
            PXW = SA.enter_context(nc.psum_tensor("pxw", [128, 512], F32))

            G0 = c["dg"]

            def fl1(e):
                e.dma_start(ATS[:].rearrange("p (kc o) -> p kc o", kc=16),
                            aTw.ap().rearrange("(kc p) o -> p kc o", p=128)
                            ).then_inc(sem["dg"], 16)
                e.dma_start(BST[:].rearrange("p (kc o) -> p kc o", kc=1),
                            bstackT.ap().rearrange("(kc p) o -> p kc o", p=128)
                            ).then_inc(sem["dg"], 16)
                e.dma_start(GWT[:].rearrange("p (kc o) -> p kc o", kc=16),
                            gwT.ap().rearrange("(kc p) o -> p kc o", p=128)
                            ).then_inc(sem["dg"], 16)
                e.dma_start(GBR[:], gbrep.ap()).then_inc(sem["dg"], 16)
                e.wait_ge(sem["dg"], G0 + 64)

            b.phase({"gpsimd": fl1})
            c["dg"] += 64

            M0, V0, A0 = c["mm"], c["vv"], c["aa"]

            def ftg(e):
                for tt in range(8):
                    for dc in range(16):
                        ins = e.matmul(PXL[:, 8 * tt:8 * tt + 8],
                                       XB[:, dc * NT + 128 * tt:dc * NT + 128 * tt + 128],
                                       GWT[:, dc * 8:dc * 8 + 8],
                                       start=(dc == 0), stop=(dc == 15))
                ins.then_inc(sem["mm"], 1)
                e.wait_ge(sem["vv"], V0 + 9)
                for tt in range(8):
                    ins = e.transpose(pbh[0:8, 128 * tt:128 * tt + 128],
                                      G1[:, 8 * tt:8 * tt + 8], IDN)
                ins.then_inc(sem["mm"], 1)
                for tn in range(2):
                    e.wait_ge(sem["vv"], V0 + 10 + tn)
                    ins = e.matmul(PXW[:, :], E8[:, 0:128],
                                   E8[:, 128 + 512 * tn:128 + 512 * tn + 512],
                                   start=True, stop=True)
                    ins.then_inc(sem["mm"], 1)
                for tn in range(2):
                    for dc in range(16):
                        ins = e.matmul(pb[tn][:, :],
                                       ATS[:, dc * 128:dc * 128 + 128],
                                       XB[:, dc * NT + 512 * tn:dc * NT + 512 * tn + 512],
                                       start=(dc == 0), stop=(dc == 15))
                    ins.then_inc(sem["mm"], 1)

            def fvg(e):
                def ch(ins):
                    e.wait_ge(sem["sxv"], b.sx(ins))
                    return ins

                e.wait_ge(sem["mm"], M0 + 1)
                for tt in range(8):
                    lg = SC2[:, 32 * tt:32 * tt + 8]
                    ch(e.tensor_add(lg, PXL[:, 8 * tt:8 * tt + 8], GBR[:, 0:8]))
                    ch(e.reduce_max(GS[:, 0:1], lg, axis=AX.X))
                    ch(e.tensor_scalar_mul(GS[:, 1:2], GS[:, 0:1], -1.0))
                    ch(e.tensor_scalar(SC2[:, 32 * tt + 8:32 * tt + 16], lg, GS[:, 0:1],
                                       None, ALU.is_equal))
                    ch(e.scalar_tensor_tensor(SC2[:, 32 * tt + 8:32 * tt + 16],
                                              SC2[:, 32 * tt + 8:32 * tt + 16], KILL,
                                              lg, ALU.mult, ALU.add))
                    ins = e.reduce_max(GS[:, 2:3], SC2[:, 32 * tt + 8:32 * tt + 16],
                                       axis=AX.X)
                    ins.then_inc(sem["vv"], 1)
                    e.wait_ge(sem["aa"], A0 + tt + 1)
                    ex = SC2[:, 32 * tt + 16:32 * tt + 24]
                    ch(e.tensor_scalar(SC2[:, 32 * tt + 24:32 * tt + 32], lg, GS[:, 2:3],
                                       None, ALU.is_ge))
                    ch(e.tensor_mul(ex, ex, SC2[:, 32 * tt + 24:32 * tt + 32]))
                    ch(e.reduce_sum(GS[:, 3:4], ex, axis=AX.X))
                    ch(e.reciprocal(GS[:, 4:5], GS[:, 3:4]))
                    ins = e.tensor_scalar_mul(G1[:, 8 * tt:8 * tt + 8], ex,
                                              GS[:, 4:5])
                    if tt == 7:
                        ins.then_inc(sem["vv"], 1)
                e.wait_ge(sem["mm"], M0 + 2)
                e.tensor_copy(E8[:, 128:128 + 1024], pbh[0:8, :]).then_inc(sem["vv"], 1)
                for tn in range(2):
                    e.wait_ge(sem["mm"], M0 + 3 + tn)
                    e.tensor_copy(WRB[:, 512 * tn:512 * tn + 512],
                                  PXW[:, :]).then_inc(sem["vv"], 1)
                for tn in range(2):
                    e.wait_ge(sem["mm"], M0 + 5 + tn)
                    e.tensor_mul(XAW[:, 512 * tn:512 * tn + 512], pb[tn][:, :],
                                 WRB[:, 512 * tn:512 * tn + 512]).then_inc(sem["vv"], 1)

            def fag(e):
                for tt in range(8):
                    e.wait_ge(sem["vv"], V0 + tt + 1)
                    lg = SC2[:, 32 * tt:32 * tt + 8]
                    ins = e.activation(SC2[:, 32 * tt + 16:32 * tt + 24], lg,
                                       AF.Exp, bias=GS[:, 1:2])
                    ins.then_inc(sem["aa"], 1)

            b.phase({"tensor": ftg, "vector": fvg, "scalar": fag})
            c["mm"] += 6
            c["vv"] += 14
            c["aa"] += 8

            G0, Q0b, M0, V0 = c["dg"], c["dq"], c["mm"], c["vv"]

            def fgb(e):
                for oc in range(16):
                    if oc >= 2:
                        e.wait_ge(sem["mm"], M0 + 2 * (oc - 1))
                    par = oc % 2
                    ps_ = sem["dg"] if par == 0 else sem["dq"]
                    e.dma_start(
                        WT[:, par * 2048:(par + 1) * 2048].rearrange(
                            "p (kc o) -> p kc o", kc=16),
                        wT1.ap()[:, 128 * oc:128 * (oc + 1)].rearrange(
                            "(kc p) o -> p kc o", p=128)).then_inc(ps_, 16)

            def ftb(e):
                for oc in range(16):
                    par = oc % 2
                    if par == 0:
                        e.wait_ge(sem["dg"], G0 + 16 * (oc // 2 + 1))
                    else:
                        e.wait_ge(sem["dq"], Q0b + 16 * (oc // 2 + 1))
                    if oc >= 2:
                        e.wait_ge(sem["vv"], V0 + 2 * (oc - 1))
                    for tn in range(2):
                        for dc in range(16):
                            e.matmul(pb[2 * par + tn][:, :],
                                     WT[:, par * 2048 + dc * 128:par * 2048 + dc * 128 + 128],
                                     XB[:, dc * NT + 512 * tn:dc * NT + 512 * tn + 512],
                                     start=(dc == 0), stop=False)
                        ins = e.matmul(pb[2 * par + tn][:, :],
                                       BST[:, 128 * oc:128 * oc + 128],
                                       XAW[:, 512 * tn:512 * tn + 512],
                                       start=False, stop=True)
                        ins.then_inc(sem["mm"], 1)

            def fvb(e):
                for oc in range(16):
                    par = oc % 2
                    for tn in range(2):
                        j = 2 * oc + tn
                        e.wait_ge(sem["mm"], M0 + j + 1)
                        if j >= 1:
                            e.wait_ge(sem["vv"], V0 + j)
                        sl = slice(oc * NT + 512 * tn, oc * NT + 512 * tn + 512)
                        ins = e.scalar_tensor_tensor(
                            SC2[:, 0:512], pb[2 * par + tn][:, :],
                            SDN[:, 16 + oc:16 + oc + 1], XF[:, sl],
                            ALU.add, ALU.add)
                        t = b.sx(ins)
                        e.wait_ge(sem["sxv"], t)
                        ins = e.tensor_copy(XF[:, sl], SC2[:, 0:512])
                        ins.then_inc(sem["vv"], 1)

            b.phase({"gpsimd": fgb, "tensor": ftb, "vector": fvb})
            c["dg"] += 8 * 16
            c["dq"] += 8 * 16
            c["mm"] += 32
            c["vv"] += 32

        emit_layernorm()
        dump("x4", XF[:, 0:16 * NT], F32, 16 * NT)

        S0 = c["ds"]

        def fst(e):
            e.dma_start(outx.ap().rearrange("(ch p) t -> p ch t", p=128),
                        XF[:].rearrange("p (ch t) -> p ch t", ch=16)
                        ).then_inc(sem["ds"], 16)
            e.wait_ge(sem["ds"], S0 + 16)

        b.phase({"sync": fst})
        c["ds"] += 16

    return nc


def _bf(x):
    return np.ascontiguousarray(x).astype(ml_dtypes.bfloat16)


def _host_inputs(inputs):
    f32 = np.float32
    sh = {}
    sh["wqdT"] = np.ascontiguousarray(inputs["L0_attn_Wq_down"].T).astype(f32)
    sh["wqupT"] = np.ascontiguousarray(inputs["L0_attn_Wq_up"].T).astype(f32)
    sh["wkvdT"] = np.ascontiguousarray(inputs["L0_attn_Wkv_down"].T).astype(f32)
    sh["kupw"] = np.ascontiguousarray(inputs["L0_attn_k_up"]).astype(f32)
    sh["vupflat"] = np.ascontiguousarray(
        inputs["L0_attn_v_up"].transpose(1, 0, 2).reshape(RKV, H * DH)).astype(f32)
    sh["woT0"] = np.ascontiguousarray(inputs["L0_attn_Wo"].T).astype(f32)
    sh["wg8"] = np.ascontiguousarray(np.asarray(inputs["L0_ffn_w_gate"], np.int8).reshape(16, 128, 64, 128).transpose(2, 1, 0, 3))
    sh["wu8"] = np.ascontiguousarray(np.asarray(inputs["L0_ffn_w_up"], np.int8).reshape(16, 128, 64, 128).transpose(2, 1, 0, 3))
    sh["wd8"] = np.ascontiguousarray(inputs["L0_ffn_w_down"]).astype(np.int8)
    sh["sgate"] = np.ascontiguousarray(
        inputs["L0_ffn_s_gate"].reshape(64, 128).T).astype(f32)
    sh["supv"] = np.ascontiguousarray(
        inputs["L0_ffn_s_up"].reshape(64, 128).T).astype(f32)
    sh["sdownp"] = np.ascontiguousarray(
        inputs["L0_ffn_s_down"].reshape(16, 128).T).astype(f32)
    sh["wq1"] = np.ascontiguousarray(inputs["L1_attn_Wq"]).astype(f32)
    sh["wk1"] = np.ascontiguousarray(inputs["L1_attn_Wk"]).astype(f32)
    sh["wv1"] = np.ascontiguousarray(inputs["L1_attn_Wv"]).astype(f32)
    sh["wo1"] = np.ascontiguousarray(inputs["L1_attn_Wo"]).astype(f32)
    sh["wT1"] = np.ascontiguousarray(inputs["L1_ffn_W"].T).astype(f32)
    sh["bP"] = np.ascontiguousarray(inputs["L1_ffn_b"].reshape(16, 128).T).astype(f32)
    sh["aTw"] = np.ascontiguousarray(
        inputs["L1_ffn_lora_A"].reshape(128, D).T).astype(f32)
    sh["bstackT"] = np.ascontiguousarray(
        inputs["L1_ffn_lora_B"].transpose(0, 2, 1).reshape(128, D)).astype(f32)
    sh["gwT"] = np.ascontiguousarray(inputs["L1_ffn_gate_w"].T).astype(f32)
    sh["gbrep"] = np.ascontiguousarray(
        np.tile(inputs["L1_ffn_gate_b"][None, :], (128, 1))).astype(f32)

    ki = np.arange(128)
    m4 = []
    for i in range(4):
        m4.append((np.arange(512)[None, :] < (ki[:, None] + 128 * i)))
    sh["masksw"] = np.ascontiguousarray(np.concatenate(m4, axis=1).astype(np.uint8))
    pm = np.zeros((128, 128), f32)
    for d in range(128):
        pm[(d - 1) % 128, d] = 1.0
    sh["pmatw"] = _bf(pm)
    es = np.zeros((8, 128), f32)
    for e in range(8):
        es[e, e * 16:(e + 1) * 16] = 1.0
    sh["eselw"] = _bf(es)
    sh["idnw"] = _bf(np.eye(128, dtype=f32))
    sh["ones1"] = _bf(np.ones((1, 128), f32))
    sh["onesm"] = _bf(np.ones((128, 128), f32))
    sh["onescol"] = _bf(np.ones((128, 1), f32))
    sh["invd"] = _bf(np.full((128, 1), 1.0 / D, f32))
    sh["epsap"] = np.full((1, 1), EPS, f32)

    x = np.asarray(inputs["L0_attn_hidden_states"], dtype=f32)
    percore = []
    for cc in range(NCORES):
        bidx, m = divmod(cc, 4)
        s0, s1 = stripes_of(m)
        pos = np.concatenate([np.arange(512 * s0, 512 * s0 + 512),
                              np.arange(512 * s1, 512 * s1 + 512)])
        d = {}
        d["xT"] = np.ascontiguousarray(x[bidx][pos].T).astype(f32)
        inv_freq = 1.0 / (10000.0 ** (np.arange(0, DH, 2, dtype=f32) / DH))
        freqs = pos[:, None].astype(f32) * inv_freq[None, :]
        emb = np.concatenate([freqs, freqs], axis=-1)
        cosT = np.cos(emb).T.astype(f32)
        sinT = np.sin(emb).T.astype(f32)
        d["ropek"] = _bf(np.concatenate([cosT, sinT], axis=1))
        ramp = np.zeros((128, 2048), np.float64)
        for h in range(6):
            rp = 32 * (h % 3)
            rb = 1024 * (h // 3)
            rv = (SLOPES[h] / SCALE) * (511.0 - np.arange(512, dtype=np.float64))
            hi = rv.astype(ml_dtypes.bfloat16)
            lo = (rv - hi.astype(np.float64)).astype(ml_dtypes.bfloat16)
            ramp[rp, rb:rb + 512] = hi.astype(np.float64)
            ramp[rp, rb + 512:rb + 1024] = lo.astype(np.float64)
        d["rampt"] = _bf(ramp)
        d["ropeq"] = _bf(np.concatenate([cosT * SCALE, sinT * SCALE], axis=1))
        alb = np.full((128, H * 48), KILL, f32)
        for h in range(H):
            sl = SLOPES[h]
            for sig in range(2):
                sstr = (s0, s1)[sig]
                q_off = 512 * sstr + 511
                base = h * 48 + (0 if sig == 0 else 16)
                nsp = NS[0][sig]
                ng = nsp - 4
                for i in range(nsp):
                    if i < ng:
                        jt = i
                        if jt >= 4 * sstr:
                            continue
                        kpos = 128 * jt + ki
                    else:
                        w = i - ng
                        jt = 4 * sstr + w
                        kpos = 128 * jt + ki
                    alb[:, base + i] = sl * (kpos - q_off)
        d["alibi"] = np.ascontiguousarray(alb)
        percore.append(d)
    return sh, percore


def _assemble(results):
    out = np.empty((B, L, D), np.float32)
    for cc in range(NCORES):
        bidx, m = divmod(cc, 4)
        s0, s1 = stripes_of(m)
        o = results[cc]["outx"]
        out[bidx, 512 * s0:512 * s0 + 512, :] = o[:, 0:512].T
        out[bidx, 512 * s1:512 * s1 + 512, :] = o[:, 512:1024].T
    return out


def kernel(**inputs):
    sh, percore = _host_inputs(inputs)
    nc = build_program()
    in_maps = []
    for cc in range(NCORES):
        d = dict(sh)
        d.update(percore[cc])
        in_maps.append(d)
    res = run_bass_kernel_spmd(nc, in_maps, core_ids=list(range(NCORES)))
    return _assemble(res.results)
